# version 15
# baseline (speedup 1.0000x reference)
import numpy as np
import ml_dtypes
import concourse.bass as bass
import concourse.mybir as mybir
from concourse.bass_utils import run_bass_kernel_spmd

F32 = mybir.dt.float32
BF16 = mybir.dt.bfloat16
ALU = mybir.AluOpType
AF = mybir.ActivationFunctionType
AX = mybir.AxisListType

D = 1024
S = 2048
NSEQ = 2
T = S * NSEQ
NC8 = 8
EPS = 1e-5



def _f32(a):
    return np.ascontiguousarray(a, dtype=np.float32)


def _rope_tables_L0():
    j = np.arange(0, 128, 2, dtype=np.float32)
    inv = (1.0 / (10000.0 ** (j / np.float32(128.0)))).astype(np.float32)
    t = np.arange(S, dtype=np.float32)[:, None]
    fr = (t * inv[None, :]).astype(np.float32)
    emb = np.concatenate([fr, fr], axis=1)
    cos = np.cos(emb).astype(np.float32)
    sin = np.sin(emb).astype(np.float32)
    ss = sin.copy()
    ss[:, 0:64] *= -1.0
    return _f32(cos.T), _f32(ss.T)


def _perm_L1():
    perm = np.zeros(D, dtype=np.int64)
    for j in range(D):
        h, r = j // 64, j % 64
        perm[j] = 64 * h + (2 * r if r < 32 else 2 * (r - 32) + 1)
    return perm


def _rope_tables_L1():
    ids = np.arange(D // 2, dtype=np.float32)
    theta = ((np.float32(10000.0) ** (np.float32(-2.0) * ids))
             / np.float32(D // 2)).astype(np.float32)
    t = np.arange(S, dtype=np.float32)[:, None]
    r = (t * theta[None, :]).astype(np.float32)
    cos = np.cos(r).astype(np.float32)
    sin = np.sin(r).astype(np.float32)
    C = np.zeros((D, S), dtype=np.float32)
    Ss = np.zeros((D, S), dtype=np.float32)
    for j in range(D):
        h, r_ = j // 64, j % 64
        ip = 32 * h + (r_ if r_ < 32 else r_ - 32)
        C[j] = cos[:, ip]
        Ss[j] = (-sin[:, ip]) if r_ < 32 else sin[:, ip]
    return _f32(C), _f32(Ss)


def prep_shared(inp):
    sh = {}
    for nm in ["Wq", "Wk", "Wv", "Wg", "Wo"]:
        w = np.asarray(inp[f"L0_attn_{nm}"])
        sh[f"l0_{nm.lower()}_t"] = _f32(w.T).reshape(NC8, 128, D)
    lw = np.asarray(inp["L0_attn_lepe_w"]).reshape(D, 5)
    sh["l0_lepe_w"] = _f32(lw.reshape(NC8, 128, 5).transpose(1, 0, 2))
    sh["l0_lepe_b"] = _f32(np.asarray(inp["L0_attn_lepe_b"]).reshape(NC8, 128).T)
    c0, s0 = _rope_tables_L0()
    sh["l0_ropec"] = c0
    sh["l0_ropes"] = s0
    sh["l0_ropecq"] = _f32(c0 * np.float32(QSCALE))
    sh["l0_ropesq"] = _f32(s0 * np.float32(QSCALE))
    sh["gate_w"] = _f32(np.asarray(inp["L0_ffn_w_gate"])).reshape(NC8, 128, 8)
    sh["gate_b"] = _f32(np.asarray(inp["L0_ffn_b_gate"])).reshape(1, 8)
    sh["w1_t"] = _f32(np.asarray(inp["L0_ffn_W1"])).reshape(8, NC8, 128, 4096)
    sh["b1_c"] = _f32(np.asarray(inp["L0_ffn_b1"])
                      .reshape(8, 32, 128).transpose(2, 0, 1))
    sh["w2_t"] = _f32(np.asarray(inp["L0_ffn_W2"])).reshape(8, 32, 128, D)
    sh["b2_r"] = _f32(np.asarray(inp["L0_ffn_b2"]))
    P = _perm_L1()
    g = lambda k: np.asarray(inp[f"L1_attn_{k}"])
    sh["wdq_t"] = _f32(g("Wd_q_w")).reshape(NC8, 128, 512)
    sh["bdq_c"] = _f32(g("Wd_q_b").reshape(4, 128).T)
    sh["wuq_t"] = _f32(g("Wu_q_w")[:, P]).reshape(4, 128, D)
    sh["buq_c"] = _f32(g("Wu_q_b")[P].reshape(NC8, 128).T)
    sh["wqr_t"] = _f32(g("Wq_r_w")[:, P]).reshape(4, 128, D)
    sh["bqr_c"] = _f32(g("Wq_r_b")[P].reshape(NC8, 128).T)
    sh["wkr_t"] = _f32(g("Wk_r_w")[:, P]).reshape(NC8, 128, D)
    sh["bkr_c"] = _f32(g("Wk_r_b")[P].reshape(NC8, 128).T)
    sh["wdkv_t"] = _f32(g("Wd_kv_w")).reshape(NC8, 128, 512)
    sh["bdkv_c"] = _f32(g("Wd_kv_b").reshape(4, 128).T)
    sh["wuk_t"] = _f32(g("Wu_k_w")[:, P]).reshape(4, 128, D)
    sh["buk_c"] = _f32(g("Wu_k_b")[P].reshape(NC8, 128).T)
    sh["wuv_t"] = _f32(g("Wu_v_w")).reshape(4, 128, D)
    sh["buv_r"] = _f32(g("Wu_v_b")).reshape(1, D)
    sh["wo1_t"] = _f32(g("Wo_w")).reshape(NC8, 128, D)
    sh["wob_c"] = _f32(g("Wo_b").reshape(NC8, 128).T)
    c1, s1 = _rope_tables_L1()
    sh["l1_ropec"] = c1.reshape(NC8, 128, S)
    swp = np.arange(D)
    swp = np.where((swp // 32) % 2 == 0, swp + 32, swp - 32)
    sh["l1_ropes"] = _f32(s1[swp]).reshape(NC8, 128, S)
    rm = np.asarray(inp["L1_ffn_rail_maker"])
    rup = np.asarray(inp["L1_ffn_rail_up_proj"])
    rmT = np.zeros((4, NC8, 128, 128), dtype=np.float32)
    rupT = np.zeros((4, NC8, 128, 512), dtype=np.float32)
    for h in range(4):
        for c in range(NC8):
            for r8 in range(8):
                r = 8 * c + r8
                rmT[h, c, 16 * r8:16 * r8 + 16, 16 * r8:16 * r8 + 16] = rm[h, r].T
                rupT[h, c, 16 * r8:16 * r8 + 16, 64 * r8:64 * r8 + 64] = rup[h, r].T
    sh["rmT_bd"] = rmT
    sh["rupT_bd"] = rupT
    sh["wc_t"] = _f32(np.asarray(inp["L1_ffn_w_cproj"]).T).reshape(128, 128, D)
    sh["iden_bf"] = np.eye(128, dtype=ml_dtypes.bfloat16)
    sh["ones_col_bf"] = np.ones((128, 1), dtype=ml_dtypes.bfloat16)
    sh["ones_row_bf"] = np.ones((1, 128), dtype=ml_dtypes.bfloat16)
    sh["ones_row_f"] = np.ones((1, 128), dtype=np.float32)
    sel = np.zeros((8, 8, 128), dtype=ml_dtypes.bfloat16)
    for i in range(8):
        sel[i, i, :] = 1.0
    sh["sel_bf"] = sel
    return sh


def prep_core_x(x, core):
    xc = np.concatenate([x[2 * core], x[2 * core + 1]], axis=0)
    return _f32(xc.T).reshape(NC8, 128, T)


def finish_out(out_fm_list):
    outs = []
    for o in out_fm_list:
        o2 = np.asarray(o).reshape(D, T).T
        outs.append(o2[:S])
        outs.append(o2[S:])
    return np.stack(outs, axis=0).astype(np.float32)



def emit_ln_stripe(block, B, c, col0, dst_hbm, dstbf_hbm):
    zpre, zbf, zsq = B["zpre"], B["zbf"], B["zsq"]
    mean, ms, var, r_, rtmp = (B["row_mean"], B["row_ms"], B["row_var"],
                               B["row_r"], B["row_tmp"])
    ones_c, ones_rf = B["ones_col_bf"], B["ones_row_f"]
    epsc = B["epsc"]
    pS, pQ, pR, pM = B["pS"], B["pQ"], B["pR"], B["pM"]
    ld, mm, ev, vx = B["s_ld"], B["s_mm"], B["s_ev"], B["s_vx"]
    stS, stG = B["s_stS"], B["s_stG"]

    rd_sem, rd_val = B.get("zpre_ready", (vx, None))
    if rd_val is None:
        rd_val = c["vx"]

    @block.vector
    def _(v):
        v.wait_ge(rd_sem, rd_val)
        v.wait_ge(stS, c["stS"])
        v.wait_ge(stG, c["stG"])
        v.tensor_copy(zbf[:, :, :], zpre[:, :, :]).then_inc(vx, 1)
        v.tensor_tensor(zsq[:, :, :], zpre[:, :, :], zpre[:, :, :],
                        ALU.mult).then_inc(vx, 1)
    c["vx"] += 2

    @block.tensor
    def _(t):
        t.wait_ge(ld, B.get("ld_static", 0))
        t.wait_ge(vx, c["vx"])
        for cc in range(NC8):
            t.matmul(pS[0:1, :], ones_c[:, :], zbf[:, cc, :],
                     start=(cc == 0), stop=(cc == NC8 - 1))
            mi = t.matmul(pQ[0:1, :], ones_c[:, :], zsq[:, cc, :],
                          start=(cc == 0), stop=(cc == NC8 - 1))
        mi.then_inc(mm, 1)
    c["mm"] += 1

    @block.vector
    def _(v):
        v.wait_ge(mm, c["mm"])
        v.tensor_scalar(mean[:, :], pS[0:1, :], 1.0 / D, None,
                        ALU.mult).then_inc(vx, 1)
        v.tensor_scalar(ms[:, :], pQ[0:1, :], 1.0 / D, None,
                        ALU.mult).then_inc(vx, 1)
        v.wait_ge(vx, c["vx"] + 2)
        v.tensor_tensor(rtmp[:, :], mean[:, :], mean[:, :], ALU.mult).then_inc(vx, 1)
        v.wait_ge(vx, c["vx"] + 3)
        v.tensor_tensor(var[:, :], ms[:, :], rtmp[:, :], ALU.subtract).then_inc(vx, 1)
    c["vx"] += 4

    @block.scalar
    def _(sc):
        sc.wait_ge(vx, c["vx"])
        sc.activation(rtmp[:, :], var[:, :], AF.Ln, bias=epsc[0:1, 0:1]).then_inc(ev, 1)
        sc.wait_ge(ev, c["ev"] + 1)
        sc.activation(r_[:, :], rtmp[:, :], AF.Exp, scale=-0.5).then_inc(ev, 1)
    c["ev"] += 2

    @block.tensor
    def _(t):
        t.wait_ge(ev, c["ev"])
        t.matmul(pR[:, :], ones_rf[:, :], r_[:, :], start=True, stop=True)
        t.matmul(pM[:, :], ones_rf[:, :], mean[:, :],
                 start=True, stop=True).then_inc(mm, 1)
    c["mm"] += 1

    @block.vector
    def _(v):
        v.wait_ge(mm, c["mm"])
        for cc in range(NC8):
            ci = v.tensor_tensor(zpre[:, cc, :], zpre[:, cc, :], pM[:, :], ALU.subtract)
        ci.then_inc(vx, 1)
        v.wait_ge(vx, c["vx"] + 1)
        for cc in range(NC8):
            ci = v.tensor_tensor(zpre[:, cc, :], zpre[:, cc, :], pR[:, :], ALU.mult)
        ci.then_inc(vx, 1)
    c["vx"] += 2

    @block.sync
    def _(sy):
        sy.wait_ge(vx, c["vx"])
        sy.dma_start(out=dst_hbm[:, :, col0:col0 + 512]
                     .rearrange("c p t -> p c t"), in_=zpre[:, :, :]).then_inc(stS, 16)
    c["stS"] += 16

    if dstbf_hbm is not None:
        @block.gpsimd
        def _(g):
            g.wait_ge(vx, c["vx"])
            g.dma_start(out=dstbf_hbm[:, :, col0:col0 + 512]
                        .rearrange("c p t -> p c t"), in_=zpre[:, :, :]).then_inc(stG, 16)
        c["stG"] += 16





QSCALE = float(np.float32(128.0 ** -0.5))


def emit_phase_a(nc, block, H, G, sems, P, c):
    x_fm = H["x_fm"]
    wq, wk, wv, wg, wo = (H["l0_wq_t"], H["l0_wk_t"], H["l0_wv_t"],
                          H["l0_wg_t"], H["l0_wo_t"])
    lepw_h, lepb_h = H["l0_lepe_w"], H["l0_lepe_b"]
    ropec_h, ropes_h = H["l0_ropec"], H["l0_ropes"]
    ropecq_h, ropesq_h = H["l0_ropecq"], H["l0_ropesq"]
    z1_h, z1b_h = H["z1_fm"], H["z1_bf"]
    iden, ones_col, ones_rf, ones512, epsc = (G["iden"], G["ones_col"],
                                              G["ones_rf"], G["ones512"],
                                              G["epsc"])

    import contextlib
    es = contextlib.ExitStack()
    def sb(name, shape, dtype):
        return es.enter_context(nc.sbuf_tensor(name + "_pa", shape, dtype))
    xbf = sb("xbf", [128, NC8, S], BF16)
    wsl = sb("wsl", [128, 2, NC8, 128], BF16)
    vfm = sb("vfm", [128, NC8, S + 4], BF16)
    ropeC = sb("ropeC", [128, S], BF16)
    ropeS = sb("ropeS", [128, S], BF16)
    ropeCq = sb("ropeCq", [128, S], BF16)
    ropeSq = sb("ropeSq", [128, S], BF16)
    swstg = sb("swstg", [128, 512], BF16)
    rstgA = sb("rstgA", [128, 512], F32)
    rstgB = sb("rstgB", [128, 512], F32)
    kwsl = sb("kwsl", [128, 2, 512], BF16)
    kvtm_k = sb("kvtm_k", [128, 4, 128], BF16)
    kvtm_v = sb("kvtm_v", [128, 4, 129], BF16)
    kvbf = sb("kvbf", [128, NC8, 129], BF16)
    vsbf = sb("vsbf", [1, NC8, 128], BF16)
    qw = sb("qw", [128, NC8, 512], BF16)
    gw = sb("gw", [128, NC8, 512], BF16)
    osb = sb("osb", [128, NC8, 512], F32)
    zpre = sb("zpre", [128, NC8, 512], F32)
    lepacc = zpre
    obf = sb("obf", [128, NC8, 512], BF16)
    xres = sb("xres", [128, NC8, 512], F32)
    numst = sb("numst", [128, 2, 512], F32)
    denr = sb("denr", [1, 2, 512], F32)
    denri = sb("denri", [1, 2, 512], F32)
    lepe_wc = sb("lepe_wc", [128, NC8, 5], F32)
    lepe_bc = sb("lepe_bc", [128, NC8], F32)
    zbf = sb("zbf", [128, NC8, 512], BF16)
    zsq = obf
    cstg = sb("cstg", [128, 4, 1024], BF16)
    row_mean = osb[0:1, 0, :]
    row_ms = osb[0:1, 1, :]
    row_var = osb[0:1, 2, :]
    row_r = osb[0:1, 3, :]
    row_tmp = osb[0:1, 4, :]
    pb0, pb1, pb2, pb3, pb4, pb5, pb6, pb7 = P[:8]
    ld, mm = sems["ld"], sems["mm"]
    glt, glx, gl0, gl1 = sems["glt"], sems["glx"], sems["gl0"], sems["gl1"]
    ev, vx = sems["ev"], sems["vx"]
    stS, stG = sems["stS"], sems["stG"]
    cld, cst = sems["cld"], sems["cst"]
    w2_h, w2bf_h = H["w2_t"], H["w2_bf"]
    with es:
        GL = (gl0, gl1)

        cast_state = {"k": 0}

        def cast_hook():
            k = cast_state["k"]
            if k >= 64:
                return
            cast_state["k"] = k + 1
            e, q, j = k // 8, (k // 2) % 4, k % 2
            m0 = 8 * q + 4 * j

            @block.gpsimd
            def _(g, e=e, m0=m0, k=k):
                g.wait_ge(cst, 16 * k)
                g.dma_start(out=cstg[:, :, :],
                            in_=w2_h[e, m0:m0 + 4, :, :]
                            .rearrange("m p d -> p m d")).then_inc(cld, 16)
            c["cld"] += 16

            @block.sync
            def _(sy, e=e, m0=m0, k=k):
                sy.wait_ge(cld, 16 * (k + 1))
                sy.dma_start(out=w2bf_h[e, m0:m0 + 4, :, :]
                             .rearrange("m p d -> p m d"),
                             in_=cstg[:, :, :]).then_inc(cst, 16)
            c["cst"] += 16
        A_LD = c["ld"] + 32
        B = dict(zpre=zpre, zbf=zbf, zsq=zsq, row_mean=row_mean, row_ms=row_ms,
                 row_var=row_var, row_r=row_r, row_tmp=row_tmp,
                 ones_col_bf=ones_col, ones_row_f=ones_rf, epsc=epsc,
                 pS=pb6, pQ=pb7, pR=pb3, pM=pb4, ld_static=G["LD0"],
                 s_ld=ld, s_mm=mm, s_ev=ev, s_vx=vx, s_stS=stS, s_stG=stG)
        pbt = pb2.bitcast(BF16)

        @block.sync
        def _(sy):
            sy.dma_start(out=lepe_wc[:, :, :], in_=lepw_h).then_inc(ld, 16)
            sy.dma_start(out=lepe_bc[:, :], in_=lepb_h).then_inc(ld, 16)
        c["ld"] += 32

        @block.gpsimd
        def _(g):
            g.dma_start(out=ropeC[:, :], in_=ropec_h).then_inc(glt, 16)
            g.dma_start(out=ropeS[:, :], in_=ropes_h).then_inc(glt, 16)
            g.dma_start(out=ropeCq[:, :], in_=ropecq_h).then_inc(glt, 16)
            g.dma_start(out=ropeSq[:, :], in_=ropesq_h).then_inc(glt, 16)
        c["glt"] += 64
        GLT_A = c["glt"]

        @block.vector
        def _(v):
            v.memset(vfm[:, :, 0:2], 0.0)
            v.memset(vfm[:, :, S + 2:S + 4], 0.0)
            mi = v.memset(kvtm_v[:, :, 128:129], 1.0)
            mi.then_inc(vx, 1)
        c["vx"] += 1
        VXA = c["vx"]

        wsl_free_mm = [0, 0]

        pband_free = {"vx": 0}

        def rope_psum(pband, tblC, tblS, w, dst_ap):
            @block.vector
            def _(v, pband=pband, w=w, dst_ap=dst_ap, tblC=tblC, tblS=tblS):
                v.wait_ge(mm, c["mm"])
                v.wait_ge(glt, GLT_A)
                v.tensor_tensor(rstgA[:, :], pband[:, :],
                                tblC[:, 512 * w:512 * w + 512],
                                ALU.mult).then_inc(vx, 1)
                v.tensor_tensor(rstgB[0:64, :], pband[64:128, :],
                                tblS[0:64, 512 * w:512 * w + 512],
                                ALU.mult).then_inc(vx, 1)
                v.tensor_tensor(rstgB[64:128, :], pband[0:64, :],
                                tblS[64:128, 512 * w:512 * w + 512],
                                ALU.mult).then_inc(vx, 1)
                v.wait_ge(vx, c["vx"] + 3)
                v.tensor_tensor(dst_ap, rstgA[:, :], rstgB[:, :],
                                ALU.add).then_inc(vx, 1)
            pband_free["vx"] = c["vx"] + 3
            c["vx"] += 4

        for s in range(NSEQ):
            @block.gpsimd
            def _(g, s=s):
                if s == 1:
                    g.wait_ge(mm, c["mm"])
                g.dma_start(out=xbf[:, :, :],
                            in_=x_fm[:, :, s * S:(s + 1) * S]
                            .rearrange("c p t -> p c t")).then_inc(glx, 16)
            c["glx"] += 16

            for oc in range(NC8):
                b = oc % 2

                @block.gpsimd
                def _(g, oc=oc, b=b):
                    g.wait_ge(mm, wsl_free_mm[b])
                    g.dma_start(out=wsl[:, b, :, :],
                                in_=wv[:, :, 128 * oc:128 * oc + 128]
                                .rearrange("c p o -> p c o")).then_inc(GL[b], 16)
                c["gl%d" % b] += 16
                cast_hook()

                vx_base = c["vx"]

                @block.tensor
                def _(t, oc=oc, b=b, vx_base=vx_base):
                    t.wait_ge(GL[b], c["gl%d" % b])
                    t.wait_ge(glx, c["glx"])
                    t.wait_ge(vx, vx_base)
                    for w in range(4):
                        pband = pb0 if (w % 2 == 0) else pb1
                        if w >= 2:
                            t.wait_ge(vx, vx_base + 1)
                        for cc in range(NC8):
                            mi = t.matmul(pband[:, :], wsl[:, b, cc, :],
                                          xbf[:, cc, 512 * w:512 * w + 512],
                                          start=(cc == 0), stop=(cc == NC8 - 1))
                            if cc == NC8 - 1:
                                mi.then_inc(mm, 1)
                wsl_free_mm[b] = c["mm"] + 4
                c["mm"] += 4

                @block.vector
                def _(v, oc=oc, s=s):
                    for j in range(2):
                        v.wait_ge(mm, c["mm"] - 2 + 2 * j)
                        v.tensor_copy(vfm[:, oc, 2 + 1024 * j:2 + 1024 * j + 1024],
                                      P[8][:, 0:1024]).then_inc(vx, 1)
                c["vx"] += 2

            kfree = {0: c["vx"], 1: c["vx"]}

            def emit_tkv(h, w, rope_done):
                @block.tensor
                def _(t, h=h, w=w, rope_done=rope_done):
                    t.wait_ge(ld, G["LD0"])
                    t.wait_ge(vx, rope_done)
                    t.wait_ge(ev, c["ev"])
                    for j in range(4):
                        t.transpose(pbt[:, 128 * j:128 * j + 128],
                                    kwsl[:, w % 2, 128 * j:128 * j + 128],
                                    iden[:, :])
                    for j in range(4):
                        mi = t.transpose(pbt[:, 512 + 128 * j:512 + 128 * j + 128],
                                         vfm[:, h, 2 + 512 * w + 128 * j:
                                             2 + 512 * w + 128 * j + 128],
                                         iden[:, :])
                    mi.then_inc(mm, 1)
                c["mm"] += 1

                @block.vector
                def _(v):
                    v.wait_ge(mm, c["mm"])
                    v.tensor_copy(kvtm_k[:, :, :],
                                  pbt[:, 0:512].rearrange(
                                      "p (a b) -> p a b", a=4)).then_inc(ev, 1)
                    v.tensor_copy(kvtm_v[:, :, 0:128],
                                  pbt[:, 512:1024].rearrange(
                                      "p (a b) -> p a b", a=4)).then_inc(ev, 1)
                c["ev"] += 2

                @block.tensor
                def _(t, h=h, w=w):
                    t.wait_ge(ev, c["ev"])
                    first0 = (w == 0)
                    last0 = (w == 3)
                    kvb = (pb3, pb4, pb5)[h // 3]
                    col = 129 * (h % 3)
                    vsb = (pb6, pb7)[h // 4]
                    vc = 128 * (h % 4)
                    for j in range(4):
                        first = first0 and j == 0
                        last = last0 and j == 3
                        t.matmul(kvb[:, col:col + 129], kvtm_k[:, j, :],
                                 kvtm_v[:, j, :], start=first, stop=last)
                        mi = t.matmul(vsb[0:1, vc:vc + 128], ones_col[:, :],
                                      kvtm_v[:, j, 0:128], start=first,
                                      stop=last)
                    mi.then_inc(mm, 1)
                c["mm"] += 1

            for h in range(NC8):
                b = h % 2

                @block.gpsimd
                def _(g, h=h, b=b):
                    g.wait_ge(mm, wsl_free_mm[b])
                    g.dma_start(out=wsl[:, b, :, :],
                                in_=wk[:, :, 128 * h:128 * h + 128]
                                .rearrange("c p o -> p c o")).then_inc(GL[b], 16)
                c["gl%d" % b] += 16
                cast_hook()

                pend = None
                for w in range(4):
                    pbK = (pb0, pb1)[w % 2]

                    @block.tensor
                    def _(t, h=h, b=b, w=w, pbK=pbK, need=kfree[w % 2]):
                        t.wait_ge(GL[b], c["gl%d" % b])
                        t.wait_ge(glx, c["glx"])
                        t.wait_ge(vx, need)
                        if s == 0 and h == 0 and w == 0:
                            t.wait_ge(vx, VXA)
                        for cc in range(NC8):
                            mi = t.matmul(pbK[:, :], wsl[:, b, cc, :],
                                          xbf[:, cc, 512 * w:512 * w + 512],
                                          start=(cc == 0), stop=(cc == NC8 - 1))
                            if cc == NC8 - 1:
                                mi.then_inc(mm, 1)
                    c["mm"] += 1
                    wsl_free_mm[b] = c["mm"]

                    rope_psum(pbK, ropeC, ropeS, w, kwsl[:, w % 2, :])
                    kfree[w % 2] = pband_free["vx"]
                    rope_done = c["vx"]
                    if pend is not None:
                        emit_tkv(*pend)
                    pend = (h, w, rope_done)
                emit_tkv(*pend)

            @block.vector
            def _(v):
                v.wait_ge(mm, c["mm"])
                for h in range(NC8):
                    kvb = (pb3, pb4, pb5)[h // 3]
                    col = 129 * (h % 3)
                    ci = v.tensor_copy(kvbf[:, h, :], kvb[:, col:col + 129])
                ci.then_inc(ev, 1)
                for h in range(NC8):
                    vsb = (pb6, pb7)[h // 4]
                    vc = 128 * (h % 4)
                    ci = v.tensor_copy(vsbf[0:1, h, :], vsb[0:1, vc:vc + 128])
                ci.then_inc(vx, 1)
            c["ev"] += 1
            c["vx"] += 1

            for w in range(4):
                col0 = s * S + 512 * w

                @block.sync
                def _(sy, w=w, col0=col0):
                    sy.wait_ge(vx, c["vx"])
                    sy.dma_start(out=xres[:, :, :],
                                 in_=x_fm[:, :, col0:col0 + 512]
                                 .rearrange("c p t -> p c t")).then_inc(ld, 16)
                c["ld"] += 16

                for oc in range(NC8):
                    b = oc % 2

                    @block.gpsimd
                    def _(g_, oc=oc, b=b):
                        g_.wait_ge(mm, wsl_free_mm[b])
                        g_.dma_start(out=wsl[:, b, :, :],
                                     in_=wg[:, :, 128 * oc:128 * oc + 128]
                                     .rearrange("c p o -> p c o")).then_inc(GL[b], 16)
                    c["gl%d" % b] += 16
                    cast_hook()

                    @block.tensor
                    def _(t, oc=oc, b=b, w=w):
                        t.wait_ge(GL[b], c["gl%d" % b])
                        t.wait_ge(glx, c["glx"])
                        t.wait_ge(vx, c["vx"])
                        for cc in range(NC8):
                            mi = t.matmul(pb0[:, :], wsl[:, b, cc, :],
                                          xbf[:, cc, 512 * w:512 * w + 512],
                                          start=(cc == 0), stop=(cc == NC8 - 1))
                            if cc == NC8 - 1:
                                mi.then_inc(mm, 1)
                    wsl_free_mm[b] = c["mm"] + 1
                    c["mm"] += 1

                    @block.vector
                    def _(v, oc=oc):
                        v.wait_ge(mm, c["mm"])
                        v.tensor_copy(gw[:, oc, :], pb0[:, :]).then_inc(vx, 1)
                    c["vx"] += 1

                for oc in range(NC8):
                    b = oc % 2

                    @block.gpsimd
                    def _(g_, oc=oc, b=b):
                        g_.wait_ge(mm, wsl_free_mm[b])
                        g_.dma_start(out=wsl[:, b, :, :],
                                     in_=wq[:, :, 128 * oc:128 * oc + 128]
                                     .rearrange("c p o -> p c o")).then_inc(GL[b], 16)
                    c["gl%d" % b] += 16
                    cast_hook()

                    vx_need = max(pband_free["vx"], c["vx"] - 3)

                    @block.tensor
                    def _(t, oc=oc, b=b, w=w, vx_need=vx_need):
                        t.wait_ge(GL[b], c["gl%d" % b])
                        t.wait_ge(glx, c["glx"])
                        t.wait_ge(vx, vx_need)
                        for cc in range(NC8):
                            mi = t.matmul(pb1[:, :], wsl[:, b, cc, :],
                                          xbf[:, cc, 512 * w:512 * w + 512],
                                          start=(cc == 0), stop=(cc == NC8 - 1))
                            if cc == NC8 - 1:
                                mi.then_inc(mm, 1)
                    wsl_free_mm[b] = c["mm"] + 1
                    c["mm"] += 1

                    rope_psum(pb1, ropeCq, ropeSq, w, qw[:, oc, :])

                norm_pending = []

                def emit_mala_norm(c=c):
                    h, par, vx_recip = norm_pending.pop(0)

                    @block.tensor
                    def _(t, h=h, par=par, vx_recip=vx_recip):
                        t.wait_ge(vx, vx_recip)
                        t.matmul(pb4[:, :], ones_rf[:, :], denri[:, par, :],
                                 start=True, stop=True).then_inc(mm, 1)
                    c["mm"] += 1

                    @block.vector
                    def _(v, h=h, par=par):
                        v.wait_ge(mm, c["mm"])
                        v.tensor_tensor(osb[:, h, :], numst[:, par, :], pb4[:, :],
                                        ALU.mult).then_inc(vx, 1)
                    c["vx"] += 1

                copies_done = c["vx"]
                for h in range(NC8):
                    par = h % 2

                    @block.tensor
                    def _(t, h=h, copies_done=copies_done):
                        t.wait_ge(ld, G["LD0"])
                        t.wait_ge(vx, copies_done)
                        t.matmul(pb2[:, :], kvbf[:, h, 0:128], qw[:, h, :],
                                 start=True, stop=False)
                        t.matmul(pb2[:, :], vsbf[0:1, h, :], ones512[:, :],
                                 start=False, stop=True)
                        t.matmul(pb3[0:1, :], kvbf[:, h, 128:129], qw[:, h, :],
                                 start=True, stop=True).then_inc(mm, 1)
                    c["mm"] += 1

                    if norm_pending:
                        emit_mala_norm()

                    @block.vector
                    def _(v, h=h, par=par):
                        v.wait_ge(mm, c["mm"])
                        v.tensor_scalar(denr[:, par, :], pb3[0:1, :], 2048.0, None,
                                        ALU.add).then_inc(vx, 1)
                        v.tensor_copy(numst[:, par, :], pb2[:, :]).then_inc(vx, 1)
                        v.wait_ge(vx, c["vx"] + 2)
                        v.reciprocal(denri[:, par, :],
                                     denr[:, par, :]).then_inc(vx, 1)
                    copies_done = c["vx"] + 2
                    c["vx"] += 3
                    norm_pending.append((h, par, c["vx"]))
                emit_mala_norm()

                @block.vector
                def _(v, w=w, s=s):
                    v.wait_ge(ld, A_LD)
                    v.wait_ge(stS, c["stS"])
                    v.wait_ge(stG, c["stG"])
                    v.wait_ge(vx, c["vx"])
                    base = 512 * w
                    for cc in range(NC8):
                        ci = v.tensor_scalar(lepacc[:, cc, :],
                                             vfm[:, cc, base + 0:base + 512],
                                             lepe_wc[:, cc, 0:1], lepe_bc[:, cc:cc + 1],
                                             ALU.mult, ALU.add)
                    ci.then_inc(vx, 1)
                    for tap in range(1, 5):
                        v.wait_ge(vx, c["vx"] + tap)
                        for cc in range(NC8):
                            ci = v.scalar_tensor_tensor(lepacc[:, cc, :],
                                                        vfm[:, cc, base + tap:base + tap + 512],
                                                        lepe_wc[:, cc, tap:tap + 1],
                                                        lepacc[:, cc, :],
                                                        ALU.mult, ALU.add)
                        ci.then_inc(vx, 1)
                c["vx"] += 5

                @block.vector
                def _(v):
                    v.wait_ge(vx, c["vx"])
                    for cc in range(NC8):
                        ci = v.tensor_tensor(osb[:, cc, :], osb[:, cc, :],
                                             lepacc[:, cc, :], ALU.add)
                    ci.then_inc(vx, 1)
                    v.wait_ge(vx, c["vx"] + 1)
                    for cc in range(NC8):
                        ci = v.tensor_tensor(obf[:, cc, :], osb[:, cc, :],
                                             gw[:, cc, :], ALU.mult)
                    ci.then_inc(vx, 1)
                c["vx"] += 2

                for oc in range(NC8):
                    b = oc % 2

                    @block.gpsimd
                    def _(g_, oc=oc, b=b):
                        g_.wait_ge(mm, wsl_free_mm[b])
                        g_.dma_start(out=wsl[:, b, :, :],
                                     in_=wo[:, :, 128 * oc:128 * oc + 128]
                                     .rearrange("c p o -> p c o")).then_inc(GL[b], 16)
                    c["gl%d" % b] += 16

                    @block.tensor
                    def _(t, oc=oc, b=b):
                        t.wait_ge(GL[b], c["gl%d" % b])
                        t.wait_ge(vx, c["vx"])
                        for cc in range(NC8):
                            mi = t.matmul(pb5[:, :], wsl[:, b, cc, :], obf[:, cc, :],
                                          start=(cc == 0), stop=(cc == NC8 - 1))
                            if cc == NC8 - 1:
                                mi.then_inc(mm, 1)
                    wsl_free_mm[b] = c["mm"] + 1
                    c["mm"] += 1

                    @block.vector
                    def _(v, oc=oc):
                        v.wait_ge(mm, c["mm"])
                        v.wait_ge(ld, c["ld"])
                        v.tensor_tensor(zpre[:, oc, :], pb5[:, :],
                                        xres[:, oc, :], ALU.add).then_inc(vx, 1)
                    c["vx"] += 1

                emit_ln_stripe(block, B, c, col0, z1_h, z1b_h)

        @block.sync
        def _(sy):
            sy.wait_ge(stS, c["stS"])
            sy.wait_ge(stG, c["stG"])



def emit_phase_b(nc, block, H, G, sems, P, c):
    z1_h, z1b_h = H["z1_fm"], H["z1_bf"]
    gw_h, gb_h = H["gate_w"], H["gate_b"]
    w1_h, b1_h = H["w1_t"], H["b1_c"]
    w2_h, b2_h = H["w2_t"], H["b2_r"]
    acc_h = H["acc_b"]
    z2_h, z2b_h = H["z2_fm"], H["z2_bf"]
    iden, ones_col, ones_rf, ones512, epsc, selsb = (
        G["iden"], G["ones_col"], G["ones_rf"], G["ones512"], G["epsc"],
        G["selsb"])

    import contextlib
    es = contextlib.ExitStack()
    def sb(name, shape, dtype):
        return es.enter_context(nc.sbuf_tensor(name + "_pb", shape, dtype))
    z1bf = sb("z1bf", [128, NC8, T], BF16)
    z1res = sb("z1res", [128, NC8, 512], F32)
    w1q = sb("w1q", [128, 2, NC8, 1024], BF16)
    w2q = sb("w2q", [128, 2, NC8, 1024], BF16)
    hbf = sb("hbf", [128, NC8, 512], BF16)
    ysb = sb("ysb", [128, 2, NC8, 512], F32)
    s8 = sb("s8", [8, NC8, 512], BF16)
    b1c_sb = sb("b1c_sb", [128, 8, 32], F32)
    sbc_sb = sb("sbc_sb", [128, 512], BF16)
    b2sb = sb("b2sb", [8, D], BF16)
    gwsb = sb("gwsb", [128, NC8, 8], F32)
    gbsb = sb("gbsb", [1, 8], F32)
    msb = sb("msb", [128, 1], F32)
    shsb = sb("shsb", [128, 8], F32)
    esb = sb("esb", [128, 8], F32)
    ssum = sb("ssum", [128, 1], F32)
    psb = sb("psb", [128, 1], F32)
    msk = sb("msk", [128, 8], F32)
    scb = sb("scb", [128, 8], BF16)
    zbf = hbf
    zsq = w1q[:, 0, :, 0:512]
    row_mean = z1res[0:1, 0, :]
    row_ms = z1res[0:1, 1, :]
    row_var = z1res[0:1, 2, :]
    row_r = z1res[0:1, 3, :]
    row_tmp = z1res[0:1, 4, :]
    pbG, pbT, pbB, pbH0, pbH1, pbY0, pbY1, pbX = P[:8]
    ld, mm, ev, vx = sems["ld"], sems["mm"], sems["ev"], sems["vx"]
    lr = sems["lr"]
    glA, glB, glC, glD, glE = (sems["glA"], sems["glB"], sems["glC"],
                               sems["glD"], sems["glE"])
    stS, stG, stH = sems["stS"], sems["stG"], sems["stH"]
    cst = sems["cst"]
    w2bf_h = H["w2_bf"]

    with es:
        W1S = [(glA, "glA"), (glB, "glB")]
        W2S = [(glC, "glC"), (glD, "glD")]
        pbTb = pbT.bitcast(BF16)
        NW = T // 512

        @block.sync
        def _(sy):
            sy.dma_start(out=z1bf[:, :, 0:2048],
                         in_=z1b_h[:, :, 0:2048]
                         .rearrange("c p t -> p c t")).then_inc(ld, 16)
            sy.dma_start(out=z1bf[:, :, 2048:4096],
                         in_=z1b_h[:, :, 2048:4096]
                         .rearrange("c p t -> p c t")).then_inc(ld, 16)
            sy.dma_start(out=gwsb[:, :, :],
                         in_=gw_h.rearrange("c p e -> p c e")).then_inc(ld, 16)
            sy.dma_start(out=gbsb[:, :], in_=gb_h).then_inc(ld, 16)
            sy.dma_start(out=b1c_sb[:, :, :], in_=b1_h).then_inc(ld, 16)
        c["ld"] += 80
        LD_STATIC = c["ld"]

        @block.gpsimd
        def _(g):
            g.dma_start(out=b2sb[:, :], in_=b2_h).then_inc(glE, 16)
        c["glE"] += 16

        B = dict(zpre=ysb[:, 0, :, :], zbf=zbf, zsq=zsq, row_mean=row_mean,
                 row_ms=row_ms,
                 row_var=row_var, row_r=row_r, row_tmp=row_tmp,
                 ones_col_bf=ones_col, ones_row_f=ones_rf, epsc=epsc,
                 pS=pbG, pQ=pbT, pR=pbB, pM=pbX, ld_static=LD_STATIC,
                 s_ld=ld, s_mm=mm, s_ev=ev, s_vx=vx, s_stS=stS, s_stG=stG)

        for w in range(NW):
            @block.sync
            def _(sy, w=w):
                if w >= 1:
                    sy.wait_ge(mm, c["mm"])
                sy.dma_start(out=z1res[:, :, :],
                             in_=z1_h[:, :, 512 * w:512 * w + 512]
                             .rearrange("c p t -> p c t")).then_inc(lr, 16)
            c["lr"] += 16

            for u in range(4):
                @block.tensor
                def _(t, w=w, u=u):
                    t.wait_ge(ld, LD_STATIC)
                    t.wait_ge(lr, c["lr"])
                    if u == 0 and w >= 1:
                        t.wait_ge(vx, c["vx"])
                    for cc in range(NC8):
                        t.matmul(pbG[:, 8 * u:8 * u + 8],
                                 z1res[:, cc, 128 * u:128 * u + 128],
                                 gwsb[:, cc, :],
                                 start=(cc == 0), stop=False)
                    t.matmul(pbG[:, 8 * u:8 * u + 8], ones_rf[:, :], gbsb[:, :],
                             start=False, stop=True).then_inc(mm, 1)
                c["mm"] += 1

                @block.vector
                def _(v, w=w, u=u):
                    v.wait_ge(mm, c["mm"])
                    v.reduce_max(msb[:, :], pbG[:, 8 * u:8 * u + 8],
                                 axis=AX.X).then_inc(vx, 1)
                    v.wait_ge(vx, c["vx"] + 1)
                    v.tensor_scalar(shsb[:, :], pbG[:, 8 * u:8 * u + 8],
                                    msb[:, :], None, ALU.subtract).then_inc(vx, 1)
                    v.wait_ge(vx, c["vx"] + 2)
                    v.tensor_scalar(msk[:, :], pbG[:, 8 * u:8 * u + 8],
                                    msb[:, :], None, ALU.is_ge).then_inc(vx, 1)
                c["vx"] += 3

                @block.scalar
                def _(sc, u=u):
                    sc.wait_ge(vx, c["vx"])
                    sc.activation(esb[:, :], shsb[:, :], AF.Exp).then_inc(ev, 1)
                c["ev"] += 1

                @block.vector
                def _(v, u=u):
                    v.wait_ge(ev, c["ev"])
                    v.reduce_sum(ssum[:, :], esb[:, :], axis=AX.X).then_inc(vx, 1)
                    v.wait_ge(vx, c["vx"] + 1)
                    v.reciprocal(psb[:, :], ssum[:, :]).then_inc(vx, 1)
                    v.wait_ge(vx, c["vx"] + 2)
                    v.tensor_scalar(scb[:, :], msk[:, :], psb[:, :], None,
                                    ALU.mult).then_inc(vx, 1)
                c["vx"] += 3

                @block.tensor
                def _(t, w=w, u=u):
                    t.wait_ge(vx, c["vx"])
                    if not (w == 0 and u == 0):
                        t.wait_ge(ev, c["ev"])
                    t.transpose(pbTb[0:8, 0:128], scb[:, :], iden[:, :]).then_inc(mm, 1)
                c["mm"] += 1

                @block.vector
                def _(v, w=w, u=u):
                    v.wait_ge(mm, c["mm"])
                    v.tensor_copy(s8[:, w, 128 * u:128 * u + 128],
                                  pbTb[0:8, 0:128]).then_inc(ev, 1)
                c["ev"] += 1

        w1_free_mm = [0, 0]
        w2_free_mm = [0, 0]
        STP = (stG, stH)
        STPN = ("stG", "stH")
        ysb_free = [0, 0]
        it = 0
        for e in range(8):
            for q in range(4):
                slot = q % 2

                @block.gpsimd
                def _(g, e=e, q=q, slot=slot):
                    g.wait_ge(mm, w1_free_mm[slot])
                    g.dma_start(out=w1q[:, slot, :, :],
                                in_=w1_h[e, :, :, 1024 * q:1024 * q + 1024]
                                .rearrange("c p m -> p c m")).then_inc(W1S[slot][0], 16)
                c[W1S[slot][1]] += 16

                @block.sync
                def _(sy, e=e, q=q, slot=slot):
                    sy.wait_ge(mm, w2_free_mm[slot])
                    sy.wait_ge(cst, 16 * (2 * (4 * e + q) + 2))
                    sy.dma_start(out=w2q[:, slot, :, :],
                                 in_=w2bf_h[e, 8 * q:8 * q + 8, :, :]
                                 .rearrange("m p d -> p m d")).then_inc(W2S[slot][0], 16)
                c[W2S[slot][1]] += 16

                HB = (pbH0, pbH1, pbG, pbT)
                YB = (pbY0, pbY1, pbX)
                for w in range(NW):
                    par = it % 2
                    ev_need_h = c["ev"]

                    @block.tensor
                    def _(t, e=e, w=w):
                        t.wait_ge(ev, c["ev"])
                        t.wait_ge(vx, c["vx"])
                        t.matmul(pbB[:, :], selsb[:, e, :], s8[:, w, :],
                                 start=True, stop=True).then_inc(mm, 1)
                    c["mm"] += 1

                    @block.vector
                    def _(v):
                        v.wait_ge(mm, c["mm"])
                        v.tensor_copy(sbc_sb[:, :], pbB[:, :]).then_inc(ev, 1)
                    c["ev"] += 1

                    hev_base = c["ev"]
                    for mc in range(NC8):
                        pbh = HB[mc % 4]

                        @block.tensor
                        def _(t, e=e, q=q, w=w, mc=mc, slot=slot, pbh=pbh,
                              hev_base=hev_base, ev_need_h=ev_need_h):
                            t.wait_ge(W1S[slot][0], c[W1S[slot][1]])
                            t.wait_ge(ld, LD_STATIC)
                            if mc == 0:
                                t.wait_ge(mm, c["mm"] - 1)
                            if mc < 4:
                                t.wait_ge(ev, ev_need_h)
                            else:
                                t.wait_ge(ev, hev_base + (mc - 4) + 1)
                            for cc in range(NC8):
                                mi = t.matmul(pbh[:, :],
                                              w1q[:, slot, cc, 128 * mc:128 * mc + 128],
                                              z1bf[:, cc, 512 * w:512 * w + 512],
                                              start=(cc == 0), stop=(cc == NC8 - 1))
                            mi.then_inc(mm, 1)
                        c["mm"] += 1

                        @block.vector
                        def _(v, e=e, q=q, mc=mc, pbh=pbh):
                            v.wait_ge(mm, c["mm"])
                            v.tensor_scalar(hbf[:, mc, :], pbh[:, :],
                                            b1c_sb[:, e, 8 * q + mc:8 * q + mc + 1],
                                            0.0, ALU.add, ALU.max).then_inc(ev, 1)
                        c["ev"] += 1

                    if e == 0 and (q == 0 or q == 3):
                        @block.sync
                        def _(sy, w=w):
                            sy.wait_ge(vx, c["vx"])
                            sy.dma_start(out=z1res[:, :, :],
                                         in_=z1_h[:, :, 512 * w:512 * w + 512]
                                         .rearrange("c p t -> p c t")
                                         ).then_inc(lr, 16)
                        c["lr"] += 16

                    y_vx_base = c["vx"]
                    y_vx_after = []
                    for dc in range(NC8):
                        pby = YB[dc % 3]
                        vx_need = y_vx_base if dc < 3 else y_vx_after[dc - 3]

                        @block.tensor
                        def _(t, e=e, q=q, w=w, dc=dc, slot=slot, pby=pby,
                              vx_need=vx_need):
                            t.wait_ge(W2S[slot][0], c[W2S[slot][1]])
                            t.wait_ge(ev, c["ev"])
                            t.wait_ge(vx, vx_need)
                            for mc in range(NC8):
                                mi = t.matmul(pby[:, :],
                                              w2q[:, slot, mc, 128 * dc:128 * dc + 128],
                                              hbf[:, mc, :],
                                              start=(mc == 0), stop=(mc == NC8 - 1))
                            mi.then_inc(mm, 1)
                        c["mm"] += 1

                        @block.vector
                        def _(v, e=e, q=q, w=w, dc=dc, pby=pby, par=par):
                            v.wait_ge(mm, c["mm"])
                            v.wait_ge(ev, c["ev"])
                            v.wait_ge(STP[par], ysb_free[par])
                            if e == 0 and q == 0:
                                v.wait_ge(lr, c["lr"])
                                v.tensor_tensor(ysb[:, par, dc, :], pby[:, :],
                                                sbc_sb[:, :], ALU.mult).then_inc(vx, 1)
                                v.wait_ge(vx, c["vx"] + 1)
                                v.tensor_tensor(ysb[:, par, dc, :], ysb[:, par, dc, :],
                                                z1res[:, dc, :], ALU.add).then_inc(vx, 1)
                            else:
                                v.tensor_tensor(ysb[:, par, dc, :], pby[:, :],
                                                sbc_sb[:, :], ALU.mult).then_inc(vx, 1)
                        c["vx"] += 2 if (e == 0 and q == 0) else 1
                        y_vx_after.append(c["vx"])

                    if e == 0 and q == 3:
                        for dc in range(NC8):
                            pby = YB[dc % 3]

                            @block.tensor
                            def _(t, w=w, dc=dc, pby=pby):
                                t.wait_ge(glE, c["glE"])
                                t.wait_ge(vx, c["vx"])
                                t.matmul(pby[:, :], b2sb[:, 128 * dc:128 * dc + 128],
                                         s8[:, w, :], start=True,
                                         stop=True).then_inc(mm, 1)
                            c["mm"] += 1

                            @block.vector
                            def _(v, w=w, dc=dc, pby=pby, par=par):
                                v.wait_ge(mm, c["mm"])
                                v.wait_ge(lr, c["lr"])
                                v.tensor_tensor(ysb[:, par, dc, :], ysb[:, par, dc, :],
                                                pby[:, :], ALU.add).then_inc(vx, 1)
                            c["vx"] += 1

                    @block.gpsimd
                    def _(g, e=e, q=q, w=w, par=par):
                        g.wait_ge(vx, c["vx"])
                        if e == 0 and q == 0:
                            g.dma_start(out=acc_h[:, :, 512 * w:512 * w + 512]
                                        .rearrange("c p t -> p c t"),
                                        in_=ysb[:, par, :, :]).then_inc(STP[par], 16)
                        else:
                            g.dma_start(out=acc_h[:, :, 512 * w:512 * w + 512]
                                        .rearrange("c p t -> p c t"),
                                        in_=ysb[:, par, :, :],
                                        accum_op=ALU.add).then_inc(STP[par], 16)
                    c[STPN[par]] += 16
                    ysb_free[par] = c[STPN[par]]
                    it += 1

                w1_free_mm[slot] = c["mm"]
                w2_free_mm[slot] = c["mm"]

        for w in range(NW):
            @block.sync
            def _(sy, w=w):
                sy.wait_ge(stG, c["stG"])
                sy.wait_ge(stH, c["stH"])
                if w >= 1:
                    sy.wait_ge(stS, c["stS"])
                    sy.wait_ge(mm, c["mm"])
                sy.dma_start(out=ysb[:, 0, :, :],
                             in_=acc_h[:, :, 512 * w:512 * w + 512]
                             .rearrange("c p t -> p c t")).then_inc(lr, 16)
            c["lr"] += 16
            B["zpre_ready"] = (lr, c["lr"])
            emit_ln_stripe(block, B, c, 512 * w, z2_h, z2b_h)

        @block.sync
        def _(sy):
            sy.wait_ge(stS, c["stS"])
            sy.wait_ge(stG, c["stG"])



SC1 = float(np.float32(1.0) / np.sqrt(np.float32(16 + 512 + 512)))


def emit_phase_c(nc, block, H, G, sems, P, c):
    z2_h, z2b_h = H["z2_fm"], H["z2_bf"]
    wdq_h, bdq_h = H["wdq_t"], H["bdq_c"]
    wuq_h, buq_h = H["wuq_t"], H["buq_c"]
    wqr_h, bqr_h = H["wqr_t"], H["bqr_c"]
    wkr_h, bkr_h = H["wkr_t"], H["bkr_c"]
    wdkv_h, bdkv_h = H["wdkv_t"], H["bdkv_c"]
    wuk_h, buk_h = H["wuk_t"], H["buk_c"]
    wuv_h, buv_h = H["wuv_t"], H["buv_r"]
    wo1_h, wob_h = H["wo1_t"], H["wob_c"]
    rc_h, rs_h = H["l1_ropec"], H["l1_ropes"]
    z3_h, z3b_h = H["z3_fm"], H["z3_bf"]
    iden, ones_col, ones_rb, ones_rf, epsc = (G["iden"], G["ones_col"],
                                              G["ones_rb"], G["ones_rf"],
                                              G["epsc"])

    import contextlib
    es = contextlib.ExitStack()
    def sb(name, shape, dtype):
        return es.enter_context(nc.sbuf_tensor(name + "_pc", shape, dtype))
    z2bf = sb("z2bf", [128, NC8, S], BF16)
    ctq = sb("ctq", [128, 4, S], BF16)
    ctkv = sb("ctkv", [128, 4, S], BF16)
    kbf = sb("kbf", [128, NC8, S], BF16)
    vaug = sb("vaug", [128, 16, 16, 65], BF16)
    wsl = sb("wsl", [128, 2, NC8, 128], BF16)
    wuvsb = sb("wuvsb", [128, 4, D], BF16)
    buvsb = sb("buvsb", [1, D], BF16)
    tbl2C = sb("tbl2C", [128, 2, 512], F32)
    tbl2S = sb("tbl2S", [128, 2, 512], F32)
    rstgA = sb("rstgA", [128, 512], BF16)
    rstgB = sb("rstgB", [128, 512], BF16)
    swstg = sb("swstg", [128, 512], BF16)
    pstg = sb("pstg", [128, 512], BF16)
    ktc_st = sb("ktc_st", [128, 512], BF16)
    numst = sb("numst", [64, 2, 512], F32)
    denr = sb("denr", [1, 2, 512], F32)
    denri = sb("denri", [1, 2, 512], F32)
    obf = sb("obf", [128, NC8, 512], BF16)
    bdq = sb("bdq", [128, 4], F32)
    buq = sb("buq", [128, NC8], F32)
    bqr = sb("bqr", [128, NC8], F32)
    bkr = sb("bkr", [128, NC8], F32)
    bdkv = sb("bdkv", [128, 4], F32)
    buk = sb("buk", [128, NC8], F32)
    wob = sb("wob", [128, NC8], F32)
    zpre = sb("zpre", [128, NC8, 512], F32)
    zsq = sb("zsq", [128, NC8, 512], BF16)
    ctkvF = ctkv.bitcast(F32)
    row_mean = ctkvF[0:1, 0, 0:512]
    row_ms = ctkvF[0:1, 0, 512:1024]
    row_var = ctkvF[0:1, 1, 0:512]
    row_r = ctkvF[0:1, 1, 512:1024]
    row_tmp = ctkvF[0:1, 2, 0:512]
    pb0, pb1, pb2, pb3, pb4, pb5, pb6, pb7 = P[:8]
    ld, lr, mm, ev, vx = (sems["ld"], sems["lr"], sems["mm"], sems["ev"],
                          sems["vx"])
    glx, gl0, gl1, glt = (sems["glx"], sems["gl0"], sems["gl1"], sems["glt"])
    stS, stG = sems["stS"], sems["stG"]

    with es:
        GL = (gl0, gl1)
        GLN = ("gl0", "gl1")

        PS4a, PS4b = P[8], P[9]
        def E_ap(kt):
            return z2bf[:, (512 * kt) // S, (512 * kt) % S:(512 * kt) % S + 512]

        def E2_ap(jp):
            return z2bf[:, (1024 * jp) // S,
                        (1024 * jp) % S:(1024 * jp) % S + 1024]

        def q_ap(oc, p0=0, pn=128):
            base = 8192 + 512 * oc
            return z2bf[p0:p0 + pn, base // S, base % S:base % S + 512]

        @block.sync
        def _(sy):
            sy.dma_start(out=bdq[:, :], in_=bdq_h).then_inc(ld, 16)
            sy.dma_start(out=buq[:, :], in_=buq_h).then_inc(ld, 16)
            sy.dma_start(out=bqr[:, :], in_=bqr_h).then_inc(ld, 16)
            sy.dma_start(out=bkr[:, :], in_=bkr_h).then_inc(ld, 16)
            sy.dma_start(out=bdkv[:, :], in_=bdkv_h).then_inc(ld, 16)
            sy.dma_start(out=buk[:, :], in_=buk_h).then_inc(ld, 16)
            sy.dma_start(out=wob[:, :], in_=wob_h).then_inc(ld, 16)
        c["ld"] += 112
        LD_STATIC = c["ld"]

        @block.gpsimd
        def _(g):
            g.dma_start(out=wuvsb[:, :, :],
                        in_=wuv_h.rearrange("c p o -> p c o")).then_inc(glt, 16)
            g.dma_start(out=buvsb[:, :], in_=buv_h).then_inc(glt, 16)
        c["glt"] += 32
        GLT_C = c["glt"]

        @block.vector
        def _(v):
            for j in range(16):
                mi = v.memset(vaug[:, j, :, 64:65], 1.0)
            mi.then_inc(vx, 1)
        c["vx"] += 1
        VXC = c["vx"]

        B = dict(zpre=zpre, zbf=obf, zsq=zsq, row_mean=row_mean, row_ms=row_ms,
                 row_var=row_var, row_r=row_r, row_tmp=row_tmp,
                 ones_col_bf=ones_col, ones_row_f=ones_rf, epsc=epsc,
                 pS=pb6, pQ=pb7, pR=pb4, pM=pb5, ld_static=LD_STATIC,
                 s_ld=ld, s_mm=mm, s_ev=ev, s_vx=vx, s_stS=stS, s_stG=stG)

        wsl_free_mm = [0, 0]

        def load_wslice(src_ap, oc, b, nchunk=NC8):
            @block.gpsimd
            def _(g, src_ap=src_ap, oc=oc, b=b, nchunk=nchunk):
                g.wait_ge(mm, wsl_free_mm[b])
                g.dma_start(out=wsl[:, b, 0:nchunk, :],
                            in_=src_ap[:, :, 128 * oc:128 * oc + 128]
                            .rearrange("c p o -> p c o")).then_inc(GL[b], 16)
            c[GLN[b]] += 16

        TBL = {"i": 0, "ri": 0, "lr": [], "free": [], "sched": []}

        def tbl_issue():
            i = TBL["i"]
            if i >= len(TBL["sched"]):
                return
            oc, w = TBL["sched"][i]
            sl = i % 2
            need = TBL["free"][i - 2] if i >= 2 else 0

            @block.sync
            def _(sy, oc=oc, w=w, sl=sl, need=need):
                sy.wait_ge(vx, need)
                sy.dma_start(out=tbl2C[:, sl, :],
                             in_=rc_h[oc, :, 512 * w:512 * w + 512]).then_inc(lr, 16)
                sy.dma_start(out=tbl2S[:, sl, :],
                             in_=rs_h[oc, :, 512 * w:512 * w + 512]).then_inc(lr, 16)
            c["lr"] += 32
            TBL["lr"].append(c["lr"])
            TBL["i"] += 1

        def rope_slice(srcpre, dst_ap, addsrc):
            ri = TBL["ri"]
            sl = ri % 2
            lr_need = TBL["lr"][ri]

            @block.vector
            def _(v, sl=sl, lr_need=lr_need):
                v.wait_ge(ev, c["ev"])
                v.wait_ge(vx, c["vx"])
                v.wait_ge(lr, lr_need)
                v.tensor_tensor(rstgA[:, :], srcpre, tbl2C[:, sl, :],
                                ALU.mult).then_inc(vx, 1)
                v.tensor_tensor(rstgB[0:32, :], srcpre[32:64, :],
                                tbl2S[32:64, sl, :], ALU.mult).then_inc(vx, 1)
                v.tensor_tensor(rstgB[32:64, :], srcpre[0:32, :],
                                tbl2S[0:32, sl, :], ALU.mult).then_inc(vx, 1)
                v.tensor_tensor(rstgB[64:96, :], srcpre[96:128, :],
                                tbl2S[96:128, sl, :], ALU.mult).then_inc(vx, 1)
                v.tensor_tensor(rstgB[96:128, :], srcpre[64:96, :],
                                tbl2S[64:96, sl, :], ALU.mult).then_inc(vx, 1)
                v.wait_ge(vx, c["vx"] + 5)
                if addsrc is None:
                    v.tensor_tensor(dst_ap, rstgA[:, :], rstgB[:, :],
                                    ALU.add).then_inc(vx, 1)
                else:
                    v.tensor_tensor(rstgA[:, :], rstgA[:, :], rstgB[:, :],
                                    ALU.add).then_inc(vx, 1)
                    v.wait_ge(vx, c["vx"] + 6)
                    v.tensor_tensor(dst_ap, rstgA[:, :], addsrc,
                                    ALU.add).then_inc(vx, 1)
            TBL["free"].append(c["vx"] + 5)
            c["vx"] += 6 if addsrc is None else 7
            TBL["ri"] += 1
            tbl_issue()

        for s in range(NSEQ):
            @block.gpsimd
            def _(g, s=s):
                if s == 1:
                    g.wait_ge(mm, c["mm"])
                g.dma_start(out=z2bf[:, :, :],
                            in_=z2b_h[:, :, s * S:(s + 1) * S]
                            .rearrange("c p t -> p c t")).then_inc(glx, 16)
            c["glx"] += 16

            TBL["sched"].extend([(oc, w) for oc in range(NC8) for w in range(4)])
            TBL["sched"].extend([(oc, w) for w in range(4) for oc in range(NC8)])
            while TBL["i"] < TBL["ri"] + 2:
                tbl_issue()

            for (wsrc, dst, bias, no) in ((wdq_h, ctq, bdq, 4), (wdkv_h, ctkv, bdkv, 4)):
                for oc in range(no):
                    b = oc % 2
                    load_wslice(wsrc, oc, b)
                    ev_base = c["ev"]

                    @block.tensor
                    def _(t, oc=oc, b=b, ev_base=ev_base):
                        t.wait_ge(GL[b], c[GLN[b]])
                        t.wait_ge(glx, c["glx"])
                        t.wait_ge(ev, ev_base)
                        for w in range(4):
                            pband = pb0 if (w % 2 == 0) else pb1
                            if w >= 2:
                                t.wait_ge(ev, ev_base + 1)
                            for cc in range(NC8):
                                mi = t.matmul(pband[:, :], wsl[:, b, cc, :],
                                              z2bf[:, cc, 512 * w:512 * w + 512],
                                              start=(cc == 0), stop=(cc == NC8 - 1))
                            mi.then_inc(mm, 1)
                    wsl_free_mm[b] = c["mm"] + 4
                    c["mm"] += 4

                    @block.vector
                    def _(v, oc=oc, dst=dst, bias=bias):
                        for j in range(2):
                            v.wait_ge(mm, c["mm"] - 2 + 2 * j)
                            v.wait_ge(ld, LD_STATIC)
                            v.tensor_scalar(dst[:, oc, 1024 * j:1024 * j + 1024],
                                            P[8][:, 0:1024], bias[:, oc:oc + 1],
                                            None, ALU.add).then_inc(ev, 1)
                    c["ev"] += 2

            for oc in range(NC8):
                b = oc % 2
                load_wslice(wkr_h, oc, b)
                ev_base = c["ev"]

                @block.tensor
                def _(t, oc=oc, b=b, ev_base=ev_base):
                    t.wait_ge(GL[b], c[GLN[b]])
                    t.wait_ge(glx, c["glx"])
                    t.wait_ge(ev, ev_base)
                    for w in range(4):
                        pband = pb0 if (w % 2 == 0) else pb1
                        if w >= 2:
                            t.wait_ge(ev, ev_base + 1)
                        for cc in range(NC8):
                            mi = t.matmul(pband[:, :], wsl[:, b, cc, :],
                                          z2bf[:, cc, 512 * w:512 * w + 512],
                                          start=(cc == 0), stop=(cc == NC8 - 1))
                        mi.then_inc(mm, 1)
                wsl_free_mm[b] = c["mm"] + 4
                c["mm"] += 4

                @block.vector
                def _(v, oc=oc):
                    for j in range(2):
                        v.wait_ge(mm, c["mm"] - 2 + 2 * j)
                        v.wait_ge(ld, LD_STATIC)
                        v.tensor_scalar(kbf[:, oc, 1024 * j:1024 * j + 1024],
                                        P[8][:, 0:1024], bkr[:, oc:oc + 1],
                                        None, ALU.add).then_inc(ev, 1)
                c["ev"] += 2

            for oc in range(NC8):
                b = oc % 2
                load_wslice(wuk_h, oc, b, nchunk=4)
                for w in range(4):
                    @block.tensor
                    def _(t, oc=oc, b=b, w=w):
                        t.wait_ge(GL[b], c[GLN[b]])
                        t.wait_ge(ev, c["ev"])
                        for cc in range(4):
                            mi = t.matmul(pb2[:, :], wsl[:, b, cc, :],
                                          ctkv[:, cc, 512 * w:512 * w + 512],
                                          start=(cc == 0), stop=(cc == 3))
                        mi.then_inc(mm, 1)
                    c["mm"] += 1

                    @block.vector
                    def _(v, oc=oc, w=w):
                        v.wait_ge(mm, c["mm"])
                        v.wait_ge(vx, c["vx"])
                        v.tensor_scalar(ktc_st[:, :], pb2[:, :],
                                        buk[:, oc:oc + 1], None,
                                        ALU.add).then_inc(ev, 1)
                        v.tensor_copy(pstg[:, :],
                                      kbf[:, oc, 512 * w:512 * w + 512]).then_inc(ev, 1)
                    c["ev"] += 2

                    rope_slice(pstg[:, :],
                               kbf[:, oc, 512 * w:512 * w + 512], ktc_st[:, :])
                wsl_free_mm[b] = c["mm"]

            for j in range(16):
                for n2 in range(2):
                    pband = pb0 if (n2 == 0) else pb1

                    @block.tensor
                    def _(t, j=j, n2=n2, pband=pband, s=s):
                        t.wait_ge(glt, GLT_C)
                        t.wait_ge(ev, c["ev"])
                        if s == 0 and j == 0 and n2 == 0:
                            t.wait_ge(vx, VXC)
                        for cc in range(4):
                            t.matmul(pband[:, :], ctkv[:, cc, 128 * j:128 * j + 128],
                                     wuvsb[:, cc, 512 * n2:512 * n2 + 512],
                                     start=(cc == 0), stop=False)
                        t.matmul(pband[:, :], ones_rb[:, :],
                                 buvsb[0:1, 512 * n2:512 * n2 + 512],
                                 start=False, stop=True).then_inc(mm, 1)
                    c["mm"] += 1

                    @block.vector
                    def _(v, j=j, n2=n2, pband=pband):
                        v.wait_ge(mm, c["mm"])
                        v.tensor_copy(vaug[:, j, 8 * n2:8 * n2 + 8, 0:64],
                                      pband[:, :].rearrange(
                                          "p (a b) -> p a b", a=8)).then_inc(ev, 1)
                    c["ev"] += 1

            for w in range(4):
                col0 = s * S + 512 * w

                for oc in range(NC8):
                    b = oc % 2
                    load_wslice(wuq_h, oc, b, nchunk=4)

                    @block.tensor
                    def _(t, oc=oc, b=b, w=w):
                        t.wait_ge(GL[b], c[GLN[b]])
                        t.wait_ge(ev, c["ev"])
                        t.wait_ge(vx, c["vx"])
                        for cc in range(4):
                            mi = t.matmul(pb2[:, :], wsl[:, b, cc, :],
                                          ctq[:, cc, 512 * w:512 * w + 512],
                                          start=(cc == 0), stop=(cc == 3))
                        mi.then_inc(mm, 1)
                    wsl_free_mm[b] = c["mm"] + 1
                    c["mm"] += 1

                    @block.vector
                    def _(v, oc=oc):
                        v.wait_ge(mm, c["mm"])
                        v.wait_ge(vx, c["vx"])
                        v.tensor_scalar(ktc_st[:, :], pb2[:, :],
                                        buq[:, oc:oc + 1], None,
                                        ALU.add).then_inc(ev, 1)
                    c["ev"] += 1

                    b2_ = (oc + 1) % 2
                    load_wslice(wqr_h, oc, b2_, nchunk=4)

                    @block.tensor
                    def _(t, oc=oc, b2_=b2_, w=w):
                        t.wait_ge(GL[b2_], c[GLN[b2_]])
                        t.wait_ge(ev, c["ev"])
                        for cc in range(4):
                            mi = t.matmul(pb3[:, :], wsl[:, b2_, cc, :],
                                          ctq[:, cc, 512 * w:512 * w + 512],
                                          start=(cc == 0), stop=(cc == 3))
                        mi.then_inc(mm, 1)
                    wsl_free_mm[b2_] = c["mm"] + 1
                    c["mm"] += 1

                    @block.vector
                    def _(v, oc=oc):
                        v.wait_ge(mm, c["mm"])
                        v.wait_ge(vx, c["vx"])
                        v.tensor_scalar(pstg[:, :], pb3[:, :],
                                        bqr[:, oc:oc + 1], None,
                                        ALU.add).then_inc(ev, 1)
                    c["ev"] += 1

                    rope_slice(pstg[:, :], q_ap(oc), ktc_st[:, :])

                PAIRS = (PS4a[:, 0:1024], PS4a[:, 1024:2048],
                         PS4b[:, 0:1024])
                VX_Q = c["vx"]
                norm_pending = []

                def emit_score_norm(c=c):
                    h, par, vx_recip = norm_pending.pop(0)
                    p0_ = 64 * (h % 2)
                    hc_ = h // 2

                    @block.tensor
                    def _(t, par=par, vx_recip=vx_recip):
                        t.wait_ge(vx, vx_recip)
                        t.matmul(pb7[0:64, :], ones_rf[:, 0:64], denri[:, par, :],
                                 start=True, stop=True).then_inc(mm, 1)
                    c["mm"] += 1

                    @block.vector
                    def _(v, par=par, p0_=p0_, hc_=hc_):
                        v.wait_ge(mm, c["mm"])
                        v.tensor_tensor(obf[p0_:p0_ + 64, hc_, :], numst[:, par, :],
                                        pb7[0:64, :], ALU.mult).then_inc(vx, 1)
                    c["vx"] += 1

                copies_done = c["vx"]
                for h in range(16):
                    p0 = 64 * (h % 2)
                    hc = h // 2
                    par = h % 2
                    ev_base = c["ev"]
                    for jp in range(8):
                        pair = PAIRS[jp % 3]

                        @block.tensor
                        def _(t, h=h, jp=jp, w=w, pair=pair, p0=p0, hc=hc,
                              ev_base=ev_base):
                            t.wait_ge(vx, VX_Q)
                            t.wait_ge(ev, max(0, ev_base + jp - 2))
                            for half in range(2):
                                kt = 2 * jp + half
                                mi = t.matmul(pair[:, 512 * half:512 * half + 512],
                                              kbf[p0:p0 + 64, hc,
                                                  128 * kt:128 * kt + 128],
                                              q_ap(2 * hc + (h % 2), p0, 64),
                                              start=True, stop=True)
                            mi.then_inc(mm, 1)
                        c["mm"] += 1

                        @block.scalar
                        def _(sc, jp=jp, pair=pair):
                            sc.wait_ge(mm, c["mm"])
                            sc.activation(E2_ap(jp), pair[:, :], AF.Exp,
                                          scale=SC1).then_inc(ev, 1)
                        c["ev"] += 1

                    if norm_pending:
                        emit_score_norm()

                    @block.tensor
                    def _(t, h=h, w=w, copies_done=copies_done, ev_base=ev_base):
                        t.wait_ge(ev, ev_base + 4)
                        t.wait_ge(vx, copies_done)
                        for kt in range(8):
                            mi = t.matmul(pb6[0:65, :], vaug[:, kt, h, :],
                                          E_ap(kt), start=(kt == 0), stop=False)
                        mi.then_inc(mm, 1)
                    c["mm"] += 1

                    @block.tensor
                    def _(t, h=h, w=w):
                        t.wait_ge(ev, c["ev"])
                        for kt in range(8, 16):
                            mi = t.matmul(pb6[0:65, :], vaug[:, kt, h, :],
                                          E_ap(kt), start=False, stop=(kt == 15))
                        mi.then_inc(mm, 1)
                    c["mm"] += 1

                    @block.vector
                    def _(v, h=h, par=par):
                        v.wait_ge(mm, c["mm"])
                        v.tensor_copy(numst[:, par, :], pb6[0:64, :]).then_inc(vx, 1)
                        v.tensor_copy(denr[:, par, :], pb6[64:65, :]).then_inc(vx, 1)
                        v.wait_ge(vx, c["vx"] + 2)
                        v.reciprocal(denri[:, par, :],
                                     denr[:, par, :]).then_inc(vx, 1)
                    copies_done = c["vx"] + 2
                    c["vx"] += 3
                    norm_pending.append((h, par, c["vx"]))
                emit_score_norm()

                @block.sync
                def _(sy, w=w, col0=col0):
                    sy.wait_ge(vx, c["vx"])
                    sy.wait_ge(stS, c["stS"])
                    sy.wait_ge(stG, c["stG"])
                    sy.dma_start(out=zpre[:, :, :],
                                 in_=z2_h[:, :, col0:col0 + 512]
                                 .rearrange("c p t -> p c t")).then_inc(lr, 16)
                c["lr"] += 16

                for oc in range(NC8):
                    b = oc % 2
                    load_wslice(wo1_h, oc, b)

                    @block.tensor
                    def _(t, oc=oc, b=b):
                        t.wait_ge(GL[b], c[GLN[b]])
                        t.wait_ge(vx, c["vx"])
                        t.wait_ge(ev, c["ev"])
                        for cc in range(NC8):
                            mi = t.matmul(pb2[:, :], wsl[:, b, cc, :], obf[:, cc, :],
                                          start=(cc == 0), stop=(cc == NC8 - 1))
                        mi.then_inc(mm, 1)
                    wsl_free_mm[b] = c["mm"] + 1
                    c["mm"] += 1

                    @block.vector
                    def _(v, oc=oc):
                        v.wait_ge(mm, c["mm"])
                        v.wait_ge(lr, c["lr"])
                        v.scalar_tensor_tensor(zpre[:, oc, :], pb2[:, :],
                                               wob[:, oc:oc + 1], zpre[:, oc, :],
                                               ALU.add, ALU.add).then_inc(vx, 1)
                    c["vx"] += 1

                emit_ln_stripe(block, B, c, col0, z3_h, z3b_h)

        @block.sync
        def _(sy):
            sy.wait_ge(stS, c["stS"])
            sy.wait_ge(stG, c["stG"])



def emit_phase_d(nc, block, H, G, sems, P, c):
    z3_h, z3b_h = H["z3_fm"], H["z3_bf"]
    rm_h, rup_h, wc_h = H["rmT_bd"], H["rupT_bd"], H["wc_t"]
    h_hbm, acc_h, out_h = H["h_d"], H["acc_d"], H["out_fm"]
    ones_col, ones_rf, epsc = G["ones_col"], G["ones_rf"], G["epsc"]

    import contextlib
    es = contextlib.ExitStack()
    def sb(name, shape, dtype):
        return es.enter_context(nc.sbuf_tensor(name + "_pd", shape, dtype))
    G_sb = sb("G_sb", [128, 4, NC8, 512], BF16)
    hst = sb("hst", [128, 32, 512], BF16)
    z3w = sb("z3w", [128, 2, 4096], BF16)
    wce = sb("wce", [128, 16, D], BF16)
    he = sb("he", [128, 2, 16, 512], BF16)
    ysb = sb("ysb", [128, 2, NC8, 512], F32)
    rmsl = sb("rmsl", [128, 2, 128], BF16)
    rupsl = sb("rupsl", [128, 2, 512], BF16)
    wceF = wce.bitcast(F32)
    zpre = wceF[:, 0:NC8, :]
    zbf = he[:, 0, 0:NC8, :]
    zsq = he[:, 1, 0:NC8, :]
    z3wF = z3w.bitcast(F32)
    row_mean = z3wF[0:1, 0, 0:512]
    row_ms = z3wF[0:1, 0, 512:1024]
    row_var = z3wF[0:1, 0, 1024:1536]
    row_r = z3wF[0:1, 1, 0:512]
    row_tmp = z3wF[0:1, 1, 512:1024]
    pb0, pb1, pb2, pb3, pb4, pb5, pb6, pb7 = P[:8]
    PS4 = (P[8], P[9])
    ld, lr, mm, ev, vx = (sems["ld"], sems["lr"], sems["mm"], sems["ev"],
                          sems["vx"])
    glA, glB = sems["glA"], sems["glB"]
    stS, stG, stH = sems["stS"], sems["stG"], sems["stH"]

    with es:
        LD_STATIC = G["LD0"]

        B = dict(zpre=zpre, zbf=zbf, zsq=zsq, row_mean=row_mean, row_ms=row_ms,
                 row_var=row_var, row_r=row_r, row_tmp=row_tmp,
                 ones_col_bf=ones_col, ones_row_f=ones_rf, epsc=epsc,
                 pS=pb6, pQ=pb7, pR=pb4, pM=pb5, ld_static=LD_STATIC,
                 s_ld=ld, s_mm=mm, s_ev=ev, s_vx=vx, s_stS=stS, s_stG=stG)

        for h4 in range(4):
            for cc in range(NC8):
                sl = (h4 * NC8 + cc) % 2

                @block.gpsimd
                def _(g, h4=h4, cc=cc, sl=sl):
                    g.wait_ge(mm, max(0, c["mm"] - 1))
                    g.dma_start(out=rmsl[:, sl, :], in_=rm_h[h4, cc, :, :]).then_inc(glA, 16)
                    g.dma_start(out=rupsl[:, sl, :], in_=rup_h[h4, cc, :, :]).then_inc(glA, 16)
                c["glA"] += 32

                @block.tensor
                def _(t, h4=h4, cc=cc, sl=sl):
                    t.wait_ge(glA, c["glA"])
                    t.wait_ge(ev, c["ev"])
                    t.matmul(pb0[:, :], rmsl[:, sl, :], rupsl[:, sl, :],
                             start=True, stop=True).then_inc(mm, 1)
                c["mm"] += 1

                @block.vector
                def _(v, h4=h4, cc=cc):
                    v.wait_ge(mm, c["mm"])
                    v.tensor_copy(G_sb[:, h4, cc, :], pb0[:, :]).then_inc(ev, 1)
                c["ev"] += 1

        NW = T // 512
        for w in range(NW):
            zsl = w % 2

            @block.sync
            def _(sy, w=w, zsl=zsl):
                sy.wait_ge(mm, c["mm"] - 32)
                sy.dma_start(out=z3w[:, zsl, :].rearrange("p (c t) -> p c t", c=NC8),
                             in_=z3b_h[:, :, 512 * w:512 * w + 512]
                             .rearrange("c p t -> p c t")).then_inc(lr, 16)
            c["lr"] += 16

            for h4 in range(4):
                ev_base = c["ev"]
                for cc in range(NC8):
                    ps = PS4[cc % 2]

                    @block.tensor
                    def _(t, w=w, h4=h4, cc=cc, ps=ps, ev_base=ev_base, zsl=zsl):
                        t.wait_ge(lr, c["lr"])
                        if cc < 2:
                            t.wait_ge(ev, ev_base)
                        else:
                            t.wait_ge(ev, ev_base + (cc - 2) + 1)
                        for sub in range(4):
                            mi = t.matmul(ps[:, 512 * sub:512 * sub + 512],
                                          G_sb[:, h4, cc, 128 * sub:128 * sub + 128],
                                          z3w[:, zsl, 512 * cc:512 * cc + 512],
                                          start=True, stop=True)
                        mi.then_inc(mm, 1)
                    c["mm"] += 1

                    @block.scalar
                    def _(sc, cc=cc, ps=ps):
                        sc.wait_ge(mm, c["mm"])
                        if cc == 0:
                            sc.wait_ge(stG, c["stG"])
                        sc.activation(hst[:, 4 * cc:4 * cc + 4, :],
                                      ps[:, :].rearrange("p (a b) -> p a b", a=4),
                                      AF.Gelu).then_inc(ev, 1)
                    c["ev"] += 1

                @block.gpsimd
                def _(g, w=w, h4=h4):
                    g.wait_ge(ev, c["ev"])
                    g.dma_start(out=h_hbm[32 * h4:32 * h4 + 32, :, 512 * w:512 * w + 512]
                                .rearrange("m p t -> p m t"),
                                in_=hst[:, :, :]).then_inc(stG, 16)
                c["stG"] += 16

        D1_STG = c["stG"]
        STP = (stG, stH)
        STPN = ("stG", "stH")
        acc_free = [c["stG"], c["stH"]]
        he_free_mm = [0, 0]
        wce_free_mm = 0
        it = 0
        for ei in range(8):
            @block.gpsimd
            def _(g, ei=ei):
                g.wait_ge(mm, wce_free_mm)
                g.dma_start(out=wce[:, :, :],
                            in_=wc_h[16 * ei:16 * ei + 16, :, :]
                            .rearrange("m p d -> p m d")).then_inc(glB, 16)
            c["glB"] += 16

            for w in range(NW):
                par = it % 2

                @block.sync
                def _(sy, ei=ei, w=w, par=par):
                    sy.wait_ge(mm, he_free_mm[par])
                    sy.wait_ge(stG, D1_STG)
                    sy.dma_start(out=he[:, par, :, :],
                                 in_=h_hbm[16 * ei:16 * ei + 16, :, 512 * w:512 * w + 512]
                                 .rearrange("m p t -> p m t")).then_inc(lr, 16)
                c["lr"] += 16

                if ei == 0:
                    @block.sync
                    def _(sy, w=w, par=par):
                        sy.wait_ge(STP[par], acc_free[par])
                        sy.dma_start(out=ysb[:, par, :, :],
                                     in_=z3_h[:, :, 512 * w:512 * w + 512]
                                     .rearrange("c p t -> p c t")).then_inc(lr, 16)
                    c["lr"] += 16

                evx_base = (c["ev"], c["vx"])
                post = []
                for oc in range(NC8):
                    pby = pb2 if (oc % 2 == 0) else pb3
                    need = evx_base if oc < 2 else post[oc - 2]

                    @block.tensor
                    def _(t, ei=ei, w=w, oc=oc, pby=pby, par=par, need=need):
                        t.wait_ge(glB, c["glB"])
                        t.wait_ge(lr, c["lr"])
                        t.wait_ge(ev, need[0])
                        t.wait_ge(vx, need[1])
                        for mc in range(16):
                            mi = t.matmul(pby[:, :], wce[:, mc, 128 * oc:128 * oc + 128],
                                          he[:, par, mc, :], start=(mc == 0),
                                          stop=(mc == 15))
                        mi.then_inc(mm, 1)
                    c["mm"] += 1

                    if ei == 0:
                        @block.vector
                        def _(v, w=w, oc=oc, pby=pby, par=par):
                            v.wait_ge(mm, c["mm"])
                            v.wait_ge(lr, c["lr"])
                            v.tensor_tensor(ysb[:, par, oc, :], ysb[:, par, oc, :],
                                            pby[:, :], ALU.add).then_inc(vx, 1)
                        c["vx"] += 1
                    else:
                        @block.vector
                        def _(v, w=w, oc=oc, pby=pby, par=par):
                            v.wait_ge(mm, c["mm"])
                            v.wait_ge(STP[par], acc_free[par])
                            v.tensor_copy(ysb[:, par, oc, :], pby[:, :]).then_inc(ev, 1)
                        c["ev"] += 1
                    post.append((c["ev"], c["vx"]))

                @block.gpsimd
                def _(g, ei=ei, w=w, par=par):
                    g.wait_ge(vx, c["vx"])
                    g.wait_ge(ev, c["ev"])
                    if ei == 0:
                        g.dma_start(out=acc_h[:, :, 512 * w:512 * w + 512]
                                    .rearrange("c p t -> p c t"),
                                    in_=ysb[:, par, :, :]).then_inc(STP[par], 16)
                    else:
                        g.dma_start(out=acc_h[:, :, 512 * w:512 * w + 512]
                                    .rearrange("c p t -> p c t"),
                                    in_=ysb[:, par, :, :],
                                    accum_op=ALU.add).then_inc(STP[par], 16)
                c[STPN[par]] += 16
                acc_free[par] = c[STPN[par]]
                he_free_mm[par] = c["mm"]
                it += 1
            wce_free_mm = c["mm"]

        for w in range(NW):
            @block.sync
            def _(sy, w=w):
                sy.wait_ge(stG, c["stG"])
                sy.wait_ge(stH, c["stH"])
                if w >= 1:
                    sy.wait_ge(stS, c["stS"])
                    sy.wait_ge(mm, c["mm"])
                sy.dma_start(out=zpre[:, :, :],
                             in_=acc_h[:, :, 512 * w:512 * w + 512]
                             .rearrange("c p t -> p c t")).then_inc(lr, 16)
            c["lr"] += 16
            B["zpre_ready"] = (lr, c["lr"])
            emit_ln_stripe(block, B, c, 512 * w, out_h, None)

        @block.sync
        def _(sy):
            sy.wait_ge(stS, c["stS"])
            sy.wait_ge(stG, c["stG"])
            sy.wait_ge(stH, c["stH"])



SEM_NAMES = ["ld", "lr", "mm", "ev", "vx", "glt", "glx", "gl0", "gl1",
             "glA", "glB", "glC", "glD", "glE", "stS", "stG", "stH",
             "cld", "cst"]

_EXT_IN = [
    ("x_fm", [NC8, 128, T], F32),
    ("l0_wq_t", [NC8, 128, D], F32), ("l0_wk_t", [NC8, 128, D], F32),
    ("l0_wv_t", [NC8, 128, D], F32), ("l0_wg_t", [NC8, 128, D], F32),
    ("l0_wo_t", [NC8, 128, D], F32),
    ("l0_lepe_w", [128, NC8, 5], F32), ("l0_lepe_b", [128, NC8], F32),
    ("l0_ropec", [128, S], F32), ("l0_ropes", [128, S], F32),
    ("l0_ropecq", [128, S], F32), ("l0_ropesq", [128, S], F32),
    ("iden_bf", [128, 128], BF16), ("ones_col_bf", [128, 1], BF16),
    ("ones_row_bf", [1, 128], BF16), ("ones_row_f", [1, 128], F32),
    ("sel_bf", [8, 8, 128], BF16),
    ("gate_w", [NC8, 128, 8], F32), ("gate_b", [1, 8], F32),
    ("w1_t", [8, NC8, 128, 4096], F32), ("b1_c", [128, 8, 32], F32),
    ("w2_t", [8, 32, 128, D], F32), ("b2_r", [8, D], F32),
    ("wdq_t", [NC8, 128, 512], F32), ("bdq_c", [128, 4], F32),
    ("wuq_t", [4, 128, D], F32), ("buq_c", [128, NC8], F32),
    ("wqr_t", [4, 128, D], F32), ("bqr_c", [128, NC8], F32),
    ("wkr_t", [NC8, 128, D], F32), ("bkr_c", [128, NC8], F32),
    ("wdkv_t", [NC8, 128, 512], F32), ("bdkv_c", [128, 4], F32),
    ("wuk_t", [4, 128, D], F32), ("buk_c", [128, NC8], F32),
    ("wuv_t", [4, 128, D], F32), ("buv_r", [1, D], F32),
    ("wo1_t", [NC8, 128, D], F32), ("wob_c", [128, NC8], F32),
    ("l1_ropec", [NC8, 128, S], F32), ("l1_ropes", [NC8, 128, S], F32),
    ("rmT_bd", [4, NC8, 128, 128], F32), ("rupT_bd", [4, NC8, 128, 512], F32),
    ("wc_t", [128, 128, D], F32),
]

_INTERNAL = [
    ("z1_fm", [NC8, 128, T], F32), ("z1_bf", [NC8, 128, T], BF16),
    ("acc_b", [NC8, 128, T], F32),
    ("z2_fm", [NC8, 128, T], F32), ("z2_bf", [NC8, 128, T], BF16),
    ("z3_fm", [NC8, 128, T], F32), ("z3_bf", [NC8, 128, T], BF16),
    ("h_d", [128, 128, T], BF16), ("acc_d", [NC8, 128, T], F32),
    ("w2_bf", [8, 32, 128, D], BF16),
]


def emit_barrier(block, sems, c):
    for eng in ("sync", "gpsimd", "scalar", "vector", "tensor"):
        def _f(e):
            for nm in SEM_NAMES:
                e.wait_ge(sems[nm], c[nm])
        getattr(block, eng)(_f)


def build_mega():
    nc = bass.Bass("TRN2", target_bir_lowering=False)
    H = {}
    for nm, shp, dt in _EXT_IN:
        H[nm] = nc.dram_tensor(nm, shp, dt, kind="ExternalInput").ap()
    for nm, shp, dt in _INTERNAL:
        H[nm] = nc.dram_tensor(nm, shp, dt, kind="Internal").ap()
    H["out_fm"] = nc.dram_tensor("out_fm", [NC8, 128, T], F32,
                                 kind="ExternalOutput").ap()
    import contextlib
    es = contextlib.ExitStack()
    PS4 = [es.enter_context(nc.psum_tensor("ps4%d" % j, [128, 2048], F32))
           for j in range(2)]
    P = [PS4[i // 4][:, 512 * (i % 4):512 * (i % 4) + 512] for i in range(8)]
    P = list(P) + PS4
    sems = {nm: es.enter_context(nc.semaphore(nm)) for nm in SEM_NAMES}
    G = {
        "iden": es.enter_context(nc.sbuf_tensor("g_iden", [128, 128], BF16)),
        "ones_col": es.enter_context(nc.sbuf_tensor("g_onc", [128, 1], BF16)),
        "ones_rb": es.enter_context(nc.sbuf_tensor("g_onrb", [1, 128], BF16)),
        "ones_rf": es.enter_context(nc.sbuf_tensor("g_onrf", [1, 128], F32)),
        "selsb": es.enter_context(nc.sbuf_tensor("g_sel", [8, 8, 128], BF16)),
        "ones512": es.enter_context(nc.sbuf_tensor("g_on512", [1, 512], BF16)),
        "epsc": es.enter_context(nc.sbuf_tensor("g_epsc", [1, 1], F32)),
    }
    with es, nc.Block() as block:
        c = {nm: 0 for nm in SEM_NAMES}

        @block.sync
        def _(sy):
            sy.dma_start(out=G["iden"][:, :], in_=H["iden_bf"]).then_inc(sems["ld"], 16)
            sy.dma_start(out=G["ones_col"][:, :], in_=H["ones_col_bf"]).then_inc(sems["ld"], 16)
            sy.dma_start(out=G["ones_rb"][:, :], in_=H["ones_row_bf"]).then_inc(sems["ld"], 16)
            sy.dma_start(out=G["ones_rf"][:, :], in_=H["ones_row_f"]).then_inc(sems["ld"], 16)
            sy.dma_start(out=G["selsb"][:, :, :], in_=H["sel_bf"]).then_inc(sems["ld"], 16)
        c["ld"] += 80
        G["LD0"] = c["ld"]

        @block.vector
        def _(v):
            v.memset(G["ones512"][:, :], 1.0)
            v.memset(G["epsc"][:, :], EPS).then_inc(sems["vx"], 1)
        c["vx"] += 1

        emit_phase_a(nc, block, H, G, sems, P, c)
        emit_barrier(block, sems, c)
        emit_phase_b(nc, block, H, G, sems, P, c)
        emit_barrier(block, sems, c)
        emit_phase_c(nc, block, H, G, sems, P, c)
        emit_barrier(block, sems, c)
        emit_phase_d(nc, block, H, G, sems, P, c)
    return nc


def kernel(**inputs):
    ncores = 8
    sh = prep_shared(inputs)
    x = np.asarray(inputs["L0_attn_hidden_states"])
    cores = list(range(ncores))
    nc = build_mega()
    maps = [dict(sh, x_fm=prep_core_x(x, cc)) for cc in cores]
    res = run_bass_kernel_spmd(nc, maps, core_ids=cores)
    outs = [res.results[cc]["out_fm"] for cc in cores]
    return finish_out(outs)


# version 16
# speedup vs baseline: 1.0462x; 1.0462x over previous
import numpy as np
import ml_dtypes
import concourse.bass as bass
import concourse.mybir as mybir
from concourse.bass_utils import run_bass_kernel_spmd

F32 = mybir.dt.float32
BF16 = mybir.dt.bfloat16
ALU = mybir.AluOpType
AF = mybir.ActivationFunctionType
AX = mybir.AxisListType

D = 1024
S = 2048
NSEQ = 2
T = S * NSEQ
NC8 = 8
EPS = 1e-5



def _f32(a):
    return np.ascontiguousarray(a, dtype=np.float32)


def _rope_tables_L0():
    j = np.arange(0, 128, 2, dtype=np.float32)
    inv = (1.0 / (10000.0 ** (j / np.float32(128.0)))).astype(np.float32)
    t = np.arange(S, dtype=np.float32)[:, None]
    fr = (t * inv[None, :]).astype(np.float32)
    emb = np.concatenate([fr, fr], axis=1)
    cos = np.cos(emb).astype(np.float32)
    sin = np.sin(emb).astype(np.float32)
    ss = sin.copy()
    ss[:, 0:64] *= -1.0
    return _f32(cos.T), _f32(ss.T)


def _perm_L1():
    perm = np.zeros(D, dtype=np.int64)
    for j in range(D):
        h, r = j // 64, j % 64
        perm[j] = 64 * h + (2 * r if r < 32 else 2 * (r - 32) + 1)
    return perm


def _rope_tables_L1():
    ids = np.arange(D // 2, dtype=np.float32)
    theta = ((np.float32(10000.0) ** (np.float32(-2.0) * ids))
             / np.float32(D // 2)).astype(np.float32)
    t = np.arange(S, dtype=np.float32)[:, None]
    r = (t * theta[None, :]).astype(np.float32)
    cos = np.cos(r).astype(np.float32)
    sin = np.sin(r).astype(np.float32)
    C = np.zeros((D, S), dtype=np.float32)
    Ss = np.zeros((D, S), dtype=np.float32)
    for j in range(D):
        h, r_ = j // 64, j % 64
        ip = 32 * h + (r_ if r_ < 32 else r_ - 32)
        C[j] = cos[:, ip]
        Ss[j] = (-sin[:, ip]) if r_ < 32 else sin[:, ip]
    return _f32(C), _f32(Ss)


def prep_shared(inp):
    sh = {}
    for nm in ["Wq", "Wk", "Wv", "Wg", "Wo"]:
        w = np.asarray(inp[f"L0_attn_{nm}"])
        sh[f"l0_{nm.lower()}_t"] = _f32(w.T).reshape(NC8, 128, D)
    lw = np.asarray(inp["L0_attn_lepe_w"]).reshape(D, 5)
    sh["l0_lepe_w"] = _f32(lw.reshape(NC8, 128, 5).transpose(1, 0, 2))
    sh["l0_lepe_b"] = _f32(np.asarray(inp["L0_attn_lepe_b"]).reshape(NC8, 128).T)
    c0, s0 = _rope_tables_L0()
    sh["l0_ropec"] = c0
    sh["l0_ropes"] = s0
    sh["l0_ropecq"] = _f32(c0 * np.float32(QSCALE))
    sh["l0_ropesq"] = _f32(s0 * np.float32(QSCALE))
    sh["gate_w"] = _f32(np.asarray(inp["L0_ffn_w_gate"])).reshape(NC8, 128, 8)
    sh["gate_b"] = _f32(np.asarray(inp["L0_ffn_b_gate"])).reshape(1, 8)
    sh["w1_t"] = _f32(np.asarray(inp["L0_ffn_W1"])).reshape(8, NC8, 128, 4096)
    sh["b1_c"] = _f32(np.asarray(inp["L0_ffn_b1"])
                      .reshape(8, 32, 128).transpose(2, 0, 1))
    sh["w2_t"] = _f32(np.asarray(inp["L0_ffn_W2"])).reshape(8, 32, 128, D)
    sh["b2_r"] = _f32(np.asarray(inp["L0_ffn_b2"]))
    P = _perm_L1()
    g = lambda k: np.asarray(inp[f"L1_attn_{k}"])
    sh["wdq_t"] = _f32(g("Wd_q_w")).reshape(NC8, 128, 512)
    sh["bdq_c"] = _f32(g("Wd_q_b").reshape(4, 128).T)
    sh["wuq_t"] = _f32(g("Wu_q_w")[:, P]).reshape(4, 128, D)
    sh["buq_c"] = _f32(g("Wu_q_b")[P].reshape(NC8, 128).T)
    sh["wqr_t"] = _f32(g("Wq_r_w")[:, P]).reshape(4, 128, D)
    sh["bqr_c"] = _f32(g("Wq_r_b")[P].reshape(NC8, 128).T)
    sh["wkr_t"] = _f32(g("Wk_r_w")[:, P]).reshape(NC8, 128, D)
    sh["bkr_c"] = _f32(g("Wk_r_b")[P].reshape(NC8, 128).T)
    sh["wdkv_t"] = _f32(g("Wd_kv_w")).reshape(NC8, 128, 512)
    sh["bdkv_c"] = _f32(g("Wd_kv_b").reshape(4, 128).T)
    sh["wuk_t"] = _f32(g("Wu_k_w")[:, P]).reshape(4, 128, D)
    sh["buk_c"] = _f32(g("Wu_k_b")[P].reshape(NC8, 128).T)
    sh["wuv_t"] = _f32(g("Wu_v_w")).reshape(4, 128, D)
    sh["buv_r"] = _f32(g("Wu_v_b")).reshape(1, D)
    sh["wo1_t"] = _f32(g("Wo_w")).reshape(NC8, 128, D)
    sh["wob_c"] = _f32(g("Wo_b").reshape(NC8, 128).T)
    c1, s1 = _rope_tables_L1()
    sh["l1_ropec"] = c1.reshape(NC8, 128, S)
    swp = np.arange(D)
    swp = np.where((swp // 32) % 2 == 0, swp + 32, swp - 32)
    sh["l1_ropes"] = _f32(s1[swp]).reshape(NC8, 128, S)
    rm = np.asarray(inp["L1_ffn_rail_maker"])
    rup = np.asarray(inp["L1_ffn_rail_up_proj"])
    rmT = np.zeros((4, NC8, 128, 128), dtype=np.float32)
    rupT = np.zeros((4, NC8, 128, 512), dtype=np.float32)
    for h in range(4):
        for c in range(NC8):
            for r8 in range(8):
                r = 8 * c + r8
                rmT[h, c, 16 * r8:16 * r8 + 16, 16 * r8:16 * r8 + 16] = rm[h, r].T
                rupT[h, c, 16 * r8:16 * r8 + 16, 64 * r8:64 * r8 + 64] = rup[h, r].T
    sh["rmT_bd"] = rmT
    sh["rupT_bd"] = rupT
    sh["wc_t"] = _f32(np.asarray(inp["L1_ffn_w_cproj"]).T).reshape(128, 128, D)
    sh["iden_bf"] = np.eye(128, dtype=ml_dtypes.bfloat16)
    sh["ones_col_bf"] = np.ones((128, 1), dtype=ml_dtypes.bfloat16)
    sh["ones_row_bf"] = np.ones((1, 128), dtype=ml_dtypes.bfloat16)
    sh["ones_row_f"] = np.ones((1, 128), dtype=np.float32)
    sel = np.zeros((8, 8, 128), dtype=ml_dtypes.bfloat16)
    for i in range(8):
        sel[i, i, :] = 1.0
    sh["sel_bf"] = sel
    return sh


def prep_core_x(x, core):
    xc = np.concatenate([x[2 * core], x[2 * core + 1]], axis=0)
    return _f32(xc.T).reshape(NC8, 128, T)


def finish_out(out_fm_list):
    outs = []
    for o in out_fm_list:
        o2 = np.asarray(o).reshape(D, T).T
        outs.append(o2[:S])
        outs.append(o2[S:])
    return np.stack(outs, axis=0).astype(np.float32)



def emit_ln_stripe(block, B, c, col0, dst_hbm, dstbf_hbm):
    zpre, zbf, zsq = B["zpre"], B["zbf"], B["zsq"]
    mean, ms, var, r_, rtmp = (B["row_mean"], B["row_ms"], B["row_var"],
                               B["row_r"], B["row_tmp"])
    ones_c, ones_rf = B["ones_col_bf"], B["ones_row_f"]
    epsc = B["epsc"]
    pS, pQ, pR, pM = B["pS"], B["pQ"], B["pR"], B["pM"]
    ld, mm, ev, vx = B["s_ld"], B["s_mm"], B["s_ev"], B["s_vx"]
    stS, stG = B["s_stS"], B["s_stG"]

    rd_sem, rd_val = B.get("zpre_ready", (vx, None))
    if rd_val is None:
        rd_val = c["vx"]

    @block.vector
    def _(v):
        v.wait_ge(rd_sem, rd_val)
        v.wait_ge(stS, c["stS"])
        v.wait_ge(stG, c["stG"])
        v.tensor_copy(zbf[:, :, :], zpre[:, :, :]).then_inc(vx, 1)
        v.tensor_tensor(zsq[:, :, :], zpre[:, :, :], zpre[:, :, :],
                        ALU.mult).then_inc(vx, 1)
    c["vx"] += 2

    @block.tensor
    def _(t):
        t.wait_ge(ld, B.get("ld_static", 0))
        t.wait_ge(vx, c["vx"])
        for cc in range(NC8):
            t.matmul(pS[0:1, :], ones_c[:, :], zbf[:, cc, :],
                     start=(cc == 0), stop=(cc == NC8 - 1))
            mi = t.matmul(pQ[0:1, :], ones_c[:, :], zsq[:, cc, :],
                          start=(cc == 0), stop=(cc == NC8 - 1))
        mi.then_inc(mm, 1)
    c["mm"] += 1

    @block.vector
    def _(v):
        v.wait_ge(mm, c["mm"])
        v.tensor_scalar(mean[:, :], pS[0:1, :], 1.0 / D, None,
                        ALU.mult).then_inc(vx, 1)
        v.tensor_scalar(ms[:, :], pQ[0:1, :], 1.0 / D, None,
                        ALU.mult).then_inc(vx, 1)
        v.wait_ge(vx, c["vx"] + 2)
        v.tensor_tensor(rtmp[:, :], mean[:, :], mean[:, :], ALU.mult).then_inc(vx, 1)
        v.wait_ge(vx, c["vx"] + 3)
        v.tensor_tensor(var[:, :], ms[:, :], rtmp[:, :], ALU.subtract).then_inc(vx, 1)
    c["vx"] += 4

    @block.scalar
    def _(sc):
        sc.wait_ge(vx, c["vx"])
        sc.activation(rtmp[:, :], var[:, :], AF.Ln, bias=epsc[0:1, 0:1]).then_inc(ev, 1)
        sc.wait_ge(ev, c["ev"] + 1)
        sc.activation(r_[:, :], rtmp[:, :], AF.Exp, scale=-0.5).then_inc(ev, 1)
    c["ev"] += 2

    @block.tensor
    def _(t):
        t.wait_ge(ev, c["ev"])
        t.matmul(pR[:, :], ones_rf[:, :], r_[:, :], start=True, stop=True)
        t.matmul(pM[:, :], ones_rf[:, :], mean[:, :],
                 start=True, stop=True).then_inc(mm, 1)
    c["mm"] += 1

    @block.vector
    def _(v):
        v.wait_ge(mm, c["mm"])
        for cc in range(NC8):
            ci = v.tensor_tensor(zpre[:, cc, :], zpre[:, cc, :], pM[:, :], ALU.subtract)
        ci.then_inc(vx, 1)
        v.wait_ge(vx, c["vx"] + 1)
        for cc in range(NC8):
            ci = v.tensor_tensor(zpre[:, cc, :], zpre[:, cc, :], pR[:, :], ALU.mult)
        ci.then_inc(vx, 1)
    c["vx"] += 2

    @block.sync
    def _(sy):
        sy.wait_ge(vx, c["vx"])
        sy.dma_start(out=dst_hbm[:, :, col0:col0 + 512]
                     .rearrange("c p t -> p c t"), in_=zpre[:, :, :]).then_inc(stS, 16)
    c["stS"] += 16

    if dstbf_hbm is not None:
        @block.gpsimd
        def _(g):
            g.wait_ge(vx, c["vx"])
            g.dma_start(out=dstbf_hbm[:, :, col0:col0 + 512]
                        .rearrange("c p t -> p c t"), in_=zpre[:, :, :]).then_inc(stG, 16)
        c["stG"] += 16





QSCALE = float(np.float32(128.0 ** -0.5))


def emit_phase_a(nc, block, H, G, sems, P, c):
    x_fm = H["x_fm"]
    wq, wk, wv, wg, wo = (H["l0_wq_t"], H["l0_wk_t"], H["l0_wv_t"],
                          H["l0_wg_t"], H["l0_wo_t"])
    lepw_h, lepb_h = H["l0_lepe_w"], H["l0_lepe_b"]
    ropec_h, ropes_h = H["l0_ropec"], H["l0_ropes"]
    ropecq_h, ropesq_h = H["l0_ropecq"], H["l0_ropesq"]
    z1_h, z1b_h = H["z1_fm"], H["z1_bf"]
    iden, ones_col, ones_rf, ones512, epsc = (G["iden"], G["ones_col"],
                                              G["ones_rf"], G["ones512"],
                                              G["epsc"])

    import contextlib
    es = contextlib.ExitStack()
    def sb(name, shape, dtype):
        return es.enter_context(nc.sbuf_tensor(name + "_pa", shape, dtype))
    xbf = sb("xbf", [128, NC8, S], BF16)
    wsl = sb("wsl", [128, 2, NC8, 128], BF16)
    vfm = sb("vfm", [128, NC8, S + 4], BF16)
    ropeC = sb("ropeC", [128, S], BF16)
    ropeS = sb("ropeS", [128, S], BF16)
    ropeCq = sb("ropeCq", [128, S], BF16)
    ropeSq = sb("ropeSq", [128, S], BF16)
    swstg = sb("swstg", [128, 512], BF16)
    rstgA = sb("rstgA", [128, 512], F32)
    rstgB = sb("rstgB", [128, 512], F32)
    kwsl = sb("kwsl", [128, 2, 512], BF16)
    kvtm_k = sb("kvtm_k", [128, 4, 128], BF16)
    kvtm_v = sb("kvtm_v", [128, 4, 129], BF16)
    kvbf = sb("kvbf", [128, NC8, 129], BF16)
    vsbf = sb("vsbf", [1, NC8, 128], BF16)
    qw = sb("qw", [128, NC8, 512], BF16)
    gw = sb("gw", [128, NC8, 512], BF16)
    osb = sb("osb", [128, NC8, 512], F32)
    zpre = sb("zpre", [128, NC8, 512], F32)
    lepacc = zpre
    obf = sb("obf", [128, NC8, 512], BF16)
    xres = sb("xres", [128, NC8, 512], F32)
    numst = sb("numst", [128, 2, 512], F32)
    denr = sb("denr", [1, 2, 512], F32)
    denri = sb("denri", [1, 2, 512], F32)
    lepe_wc = sb("lepe_wc", [128, NC8, 5], F32)
    lepe_bc = sb("lepe_bc", [128, NC8], F32)
    zbf = sb("zbf", [128, NC8, 512], BF16)
    zsq = obf
    cstg = sb("cstg", [128, 2, 2, 1024], BF16)
    row_mean = osb[0:1, 0, :]
    row_ms = osb[0:1, 1, :]
    row_var = osb[0:1, 2, :]
    row_r = osb[0:1, 3, :]
    row_tmp = osb[0:1, 4, :]
    pb0, pb1, pb2, pb3, pb4, pb5, pb6, pb7 = P[:8]
    ld, mm = sems["ld"], sems["mm"]
    glt, glx, gl0, gl1 = sems["glt"], sems["glx"], sems["gl0"], sems["gl1"]
    ev, vx = sems["ev"], sems["vx"]
    stS, stG = sems["stS"], sems["stG"]
    cld, cst = sems["cld"], sems["cst"]
    w2_h, w2bf_h = H["w2_t"], H["w2_bf"]
    with es:
        GL = (gl0, gl1)

        cast_state = {"k": 0}

        def cast_hook():
            for _rep in range(2):
                k = cast_state["k"]
                if k >= 128:
                    return
                cast_state["k"] = k + 1
                e, m0, sl = k // 16, 2 * (k % 16), k % 2

                @block.gpsimd
                def _(g, e=e, m0=m0, k=k, sl=sl):
                    g.wait_ge(cst, 16 * max(0, k - 1))
                    g.dma_start(out=cstg[:, sl, :, :],
                                in_=w2_h[e, m0:m0 + 2, :, :]
                                .rearrange("m p d -> p m d")).then_inc(cld, 16)
                c["cld"] += 16

                @block.sync
                def _(sy, e=e, m0=m0, k=k, sl=sl):
                    sy.wait_ge(cld, 16 * (k + 1))
                    sy.dma_start(out=w2bf_h[e, m0:m0 + 2, :, :]
                                 .rearrange("m p d -> p m d"),
                                 in_=cstg[:, sl, :, :]).then_inc(cst, 16)
                c["cst"] += 16
        A_LD = c["ld"] + 32
        B = dict(zpre=zpre, zbf=zbf, zsq=zsq, row_mean=row_mean, row_ms=row_ms,
                 row_var=row_var, row_r=row_r, row_tmp=row_tmp,
                 ones_col_bf=ones_col, ones_row_f=ones_rf, epsc=epsc,
                 pS=pb6, pQ=pb7, pR=pb3, pM=pb4, ld_static=G["LD0"],
                 s_ld=ld, s_mm=mm, s_ev=ev, s_vx=vx, s_stS=stS, s_stG=stG)
        pbt = pb2.bitcast(BF16)

        @block.sync
        def _(sy):
            sy.dma_start(out=lepe_wc[:, :, :], in_=lepw_h).then_inc(ld, 16)
            sy.dma_start(out=lepe_bc[:, :], in_=lepb_h).then_inc(ld, 16)
        c["ld"] += 32

        @block.gpsimd
        def _(g):
            g.dma_start(out=ropeC[:, :], in_=ropec_h).then_inc(glt, 16)
            g.dma_start(out=ropeS[:, :], in_=ropes_h).then_inc(glt, 16)
            g.dma_start(out=ropeCq[:, :], in_=ropecq_h).then_inc(glt, 16)
            g.dma_start(out=ropeSq[:, :], in_=ropesq_h).then_inc(glt, 16)
        c["glt"] += 64
        GLT_A = c["glt"]

        @block.vector
        def _(v):
            v.memset(vfm[:, :, 0:2], 0.0)
            v.memset(vfm[:, :, S + 2:S + 4], 0.0)
            mi = v.memset(kvtm_v[:, :, 128:129], 1.0)
            mi.then_inc(vx, 1)
        c["vx"] += 1
        VXA = c["vx"]

        wsl_free_mm = [0, 0]

        pband_free = {"vx": 0}

        def rope_psum(pband, tblC, tblS, w, dst_ap):
            @block.vector
            def _(v, pband=pband, w=w, dst_ap=dst_ap, tblC=tblC, tblS=tblS):
                v.wait_ge(mm, c["mm"])
                v.wait_ge(glt, GLT_A)
                v.tensor_tensor(rstgA[:, :], pband[:, :],
                                tblC[:, 512 * w:512 * w + 512],
                                ALU.mult).then_inc(vx, 1)
                v.tensor_tensor(rstgB[0:64, :], pband[64:128, :],
                                tblS[0:64, 512 * w:512 * w + 512],
                                ALU.mult).then_inc(vx, 1)
                v.tensor_tensor(rstgB[64:128, :], pband[0:64, :],
                                tblS[64:128, 512 * w:512 * w + 512],
                                ALU.mult).then_inc(vx, 1)
                v.wait_ge(vx, c["vx"] + 3)
                v.tensor_tensor(dst_ap, rstgA[:, :], rstgB[:, :],
                                ALU.add).then_inc(vx, 1)
            pband_free["vx"] = c["vx"] + 3
            c["vx"] += 4

        for s in range(NSEQ):
            @block.gpsimd
            def _(g, s=s):
                if s == 1:
                    g.wait_ge(mm, c["mm"])
                g.dma_start(out=xbf[:, :, :],
                            in_=x_fm[:, :, s * S:(s + 1) * S]
                            .rearrange("c p t -> p c t")).then_inc(glx, 16)
            c["glx"] += 16

            for oc in range(NC8):
                b = oc % 2

                @block.gpsimd
                def _(g, oc=oc, b=b):
                    g.wait_ge(mm, wsl_free_mm[b])
                    g.dma_start(out=wsl[:, b, :, :],
                                in_=wv[:, :, 128 * oc:128 * oc + 128]
                                .rearrange("c p o -> p c o")).then_inc(GL[b], 16)
                c["gl%d" % b] += 16
                cast_hook()

                vx_base = c["vx"]

                @block.tensor
                def _(t, oc=oc, b=b, vx_base=vx_base):
                    t.wait_ge(GL[b], c["gl%d" % b])
                    t.wait_ge(glx, c["glx"])
                    t.wait_ge(vx, vx_base)
                    for w in range(4):
                        pband = pb0 if (w % 2 == 0) else pb1
                        if w >= 2:
                            t.wait_ge(vx, vx_base + 1)
                        for cc in range(NC8):
                            mi = t.matmul(pband[:, :], wsl[:, b, cc, :],
                                          xbf[:, cc, 512 * w:512 * w + 512],
                                          start=(cc == 0), stop=(cc == NC8 - 1))
                            if cc == NC8 - 1:
                                mi.then_inc(mm, 1)
                wsl_free_mm[b] = c["mm"] + 4
                c["mm"] += 4

                @block.vector
                def _(v, oc=oc, s=s):
                    for j in range(2):
                        v.wait_ge(mm, c["mm"] - 2 + 2 * j)
                        v.tensor_copy(vfm[:, oc, 2 + 1024 * j:2 + 1024 * j + 1024],
                                      P[8][:, 0:1024]).then_inc(vx, 1)
                c["vx"] += 2

            kfree = {0: c["vx"], 1: c["vx"]}

            def emit_tkv(h, w, rope_done):
                @block.tensor
                def _(t, h=h, w=w, rope_done=rope_done):
                    t.wait_ge(ld, G["LD0"])
                    t.wait_ge(vx, rope_done)
                    t.wait_ge(ev, c["ev"])
                    for j in range(4):
                        t.transpose(pbt[:, 128 * j:128 * j + 128],
                                    kwsl[:, w % 2, 128 * j:128 * j + 128],
                                    iden[:, :])
                    for j in range(4):
                        mi = t.transpose(pbt[:, 512 + 128 * j:512 + 128 * j + 128],
                                         vfm[:, h, 2 + 512 * w + 128 * j:
                                             2 + 512 * w + 128 * j + 128],
                                         iden[:, :])
                    mi.then_inc(mm, 1)
                c["mm"] += 1

                @block.vector
                def _(v):
                    v.wait_ge(mm, c["mm"])
                    v.tensor_copy(kvtm_k[:, :, :],
                                  pbt[:, 0:512].rearrange(
                                      "p (a b) -> p a b", a=4)).then_inc(ev, 1)
                    v.tensor_copy(kvtm_v[:, :, 0:128],
                                  pbt[:, 512:1024].rearrange(
                                      "p (a b) -> p a b", a=4)).then_inc(ev, 1)
                c["ev"] += 2

                @block.tensor
                def _(t, h=h, w=w):
                    t.wait_ge(ev, c["ev"])
                    first0 = (w == 0)
                    last0 = (w == 3)
                    kvb = (pb3, pb4, pb5)[h // 3]
                    col = 129 * (h % 3)
                    vsb = (pb6, pb7)[h // 4]
                    vc = 128 * (h % 4)
                    for j in range(4):
                        first = first0 and j == 0
                        last = last0 and j == 3
                        t.matmul(kvb[:, col:col + 129], kvtm_k[:, j, :],
                                 kvtm_v[:, j, :], start=first, stop=last)
                        mi = t.matmul(vsb[0:1, vc:vc + 128], ones_col[:, :],
                                      kvtm_v[:, j, 0:128], start=first,
                                      stop=last)
                    mi.then_inc(mm, 1)
                c["mm"] += 1

            for h in range(NC8):
                b = h % 2

                @block.gpsimd
                def _(g, h=h, b=b):
                    g.wait_ge(mm, wsl_free_mm[b])
                    g.dma_start(out=wsl[:, b, :, :],
                                in_=wk[:, :, 128 * h:128 * h + 128]
                                .rearrange("c p o -> p c o")).then_inc(GL[b], 16)
                c["gl%d" % b] += 16
                cast_hook()

                pend = None
                for w in range(4):
                    pbK = (pb0, pb1)[w % 2]

                    @block.tensor
                    def _(t, h=h, b=b, w=w, pbK=pbK, need=kfree[w % 2]):
                        t.wait_ge(GL[b], c["gl%d" % b])
                        t.wait_ge(glx, c["glx"])
                        t.wait_ge(vx, need)
                        if s == 0 and h == 0 and w == 0:
                            t.wait_ge(vx, VXA)
                        for cc in range(NC8):
                            mi = t.matmul(pbK[:, :], wsl[:, b, cc, :],
                                          xbf[:, cc, 512 * w:512 * w + 512],
                                          start=(cc == 0), stop=(cc == NC8 - 1))
                            if cc == NC8 - 1:
                                mi.then_inc(mm, 1)
                    c["mm"] += 1
                    wsl_free_mm[b] = c["mm"]

                    rope_psum(pbK, ropeC, ropeS, w, kwsl[:, w % 2, :])
                    kfree[w % 2] = pband_free["vx"]
                    rope_done = c["vx"]
                    if pend is not None:
                        emit_tkv(*pend)
                    pend = (h, w, rope_done)
                emit_tkv(*pend)

            @block.vector
            def _(v):
                v.wait_ge(mm, c["mm"])
                for h in range(NC8):
                    kvb = (pb3, pb4, pb5)[h // 3]
                    col = 129 * (h % 3)
                    ci = v.tensor_copy(kvbf[:, h, :], kvb[:, col:col + 129])
                ci.then_inc(ev, 1)
                for h in range(NC8):
                    vsb = (pb6, pb7)[h // 4]
                    vc = 128 * (h % 4)
                    ci = v.tensor_copy(vsbf[0:1, h, :], vsb[0:1, vc:vc + 128])
                ci.then_inc(vx, 1)
            c["ev"] += 1
            c["vx"] += 1

            for w in range(4):
                col0 = s * S + 512 * w

                @block.sync
                def _(sy, w=w, col0=col0):
                    sy.wait_ge(vx, c["vx"])
                    sy.dma_start(out=xres[:, :, :],
                                 in_=x_fm[:, :, col0:col0 + 512]
                                 .rearrange("c p t -> p c t")).then_inc(ld, 16)
                c["ld"] += 16

                for oc in range(NC8):
                    b = oc % 2

                    @block.gpsimd
                    def _(g_, oc=oc, b=b):
                        g_.wait_ge(mm, wsl_free_mm[b])
                        g_.dma_start(out=wsl[:, b, :, :],
                                     in_=wg[:, :, 128 * oc:128 * oc + 128]
                                     .rearrange("c p o -> p c o")).then_inc(GL[b], 16)
                    c["gl%d" % b] += 16
                    cast_hook()

                    @block.tensor
                    def _(t, oc=oc, b=b, w=w):
                        t.wait_ge(GL[b], c["gl%d" % b])
                        t.wait_ge(glx, c["glx"])
                        t.wait_ge(vx, c["vx"])
                        for cc in range(NC8):
                            mi = t.matmul(pb0[:, :], wsl[:, b, cc, :],
                                          xbf[:, cc, 512 * w:512 * w + 512],
                                          start=(cc == 0), stop=(cc == NC8 - 1))
                            if cc == NC8 - 1:
                                mi.then_inc(mm, 1)
                    wsl_free_mm[b] = c["mm"] + 1
                    c["mm"] += 1

                    @block.vector
                    def _(v, oc=oc):
                        v.wait_ge(mm, c["mm"])
                        v.tensor_copy(gw[:, oc, :], pb0[:, :]).then_inc(vx, 1)
                    c["vx"] += 1

                for oc in range(NC8):
                    b = oc % 2

                    @block.gpsimd
                    def _(g_, oc=oc, b=b):
                        g_.wait_ge(mm, wsl_free_mm[b])
                        g_.dma_start(out=wsl[:, b, :, :],
                                     in_=wq[:, :, 128 * oc:128 * oc + 128]
                                     .rearrange("c p o -> p c o")).then_inc(GL[b], 16)
                    c["gl%d" % b] += 16
                    cast_hook()

                    vx_need = max(pband_free["vx"], c["vx"] - 3)

                    @block.tensor
                    def _(t, oc=oc, b=b, w=w, vx_need=vx_need):
                        t.wait_ge(GL[b], c["gl%d" % b])
                        t.wait_ge(glx, c["glx"])
                        t.wait_ge(vx, vx_need)
                        for cc in range(NC8):
                            mi = t.matmul(pb1[:, :], wsl[:, b, cc, :],
                                          xbf[:, cc, 512 * w:512 * w + 512],
                                          start=(cc == 0), stop=(cc == NC8 - 1))
                            if cc == NC8 - 1:
                                mi.then_inc(mm, 1)
                    wsl_free_mm[b] = c["mm"] + 1
                    c["mm"] += 1

                    rope_psum(pb1, ropeCq, ropeSq, w, qw[:, oc, :])

                norm_pending = []

                def emit_mala_norm(c=c):
                    h, par, vx_recip = norm_pending.pop(0)

                    @block.tensor
                    def _(t, h=h, par=par, vx_recip=vx_recip):
                        t.wait_ge(vx, vx_recip)
                        t.matmul(pb4[:, :], ones_rf[:, :], denri[:, par, :],
                                 start=True, stop=True).then_inc(mm, 1)
                    c["mm"] += 1

                    @block.vector
                    def _(v, h=h, par=par):
                        v.wait_ge(mm, c["mm"])
                        v.tensor_tensor(osb[:, h, :], numst[:, par, :], pb4[:, :],
                                        ALU.mult).then_inc(vx, 1)
                    c["vx"] += 1

                copies_done = c["vx"]
                for h in range(NC8):
                    par = h % 2

                    @block.tensor
                    def _(t, h=h, copies_done=copies_done):
                        t.wait_ge(ld, G["LD0"])
                        t.wait_ge(vx, copies_done)
                        t.matmul(pb2[:, :], kvbf[:, h, 0:128], qw[:, h, :],
                                 start=True, stop=False)
                        t.matmul(pb2[:, :], vsbf[0:1, h, :], ones512[:, :],
                                 start=False, stop=True)
                        t.matmul(pb3[0:1, :], kvbf[:, h, 128:129], qw[:, h, :],
                                 start=True, stop=True).then_inc(mm, 1)
                    c["mm"] += 1

                    if norm_pending:
                        emit_mala_norm()

                    @block.vector
                    def _(v, h=h, par=par):
                        v.wait_ge(mm, c["mm"])
                        v.tensor_scalar(denr[:, par, :], pb3[0:1, :], 2048.0, None,
                                        ALU.add).then_inc(vx, 1)
                        v.tensor_copy(numst[:, par, :], pb2[:, :]).then_inc(vx, 1)
                        v.wait_ge(vx, c["vx"] + 2)
                        v.reciprocal(denri[:, par, :],
                                     denr[:, par, :]).then_inc(vx, 1)
                    copies_done = c["vx"] + 2
                    c["vx"] += 3
                    norm_pending.append((h, par, c["vx"]))
                emit_mala_norm()

                @block.vector
                def _(v, w=w, s=s):
                    v.wait_ge(ld, A_LD)
                    v.wait_ge(stS, c["stS"])
                    v.wait_ge(stG, c["stG"])
                    v.wait_ge(vx, c["vx"])
                    base = 512 * w
                    for cc in range(NC8):
                        ci = v.tensor_scalar(lepacc[:, cc, :],
                                             vfm[:, cc, base + 0:base + 512],
                                             lepe_wc[:, cc, 0:1], lepe_bc[:, cc:cc + 1],
                                             ALU.mult, ALU.add)
                    ci.then_inc(vx, 1)
                    for tap in range(1, 5):
                        v.wait_ge(vx, c["vx"] + tap)
                        for cc in range(NC8):
                            ci = v.scalar_tensor_tensor(lepacc[:, cc, :],
                                                        vfm[:, cc, base + tap:base + tap + 512],
                                                        lepe_wc[:, cc, tap:tap + 1],
                                                        lepacc[:, cc, :],
                                                        ALU.mult, ALU.add)
                        ci.then_inc(vx, 1)
                c["vx"] += 5

                @block.vector
                def _(v):
                    v.wait_ge(vx, c["vx"])
                    for cc in range(NC8):
                        ci = v.tensor_tensor(osb[:, cc, :], osb[:, cc, :],
                                             lepacc[:, cc, :], ALU.add)
                    ci.then_inc(vx, 1)
                    v.wait_ge(vx, c["vx"] + 1)
                    for cc in range(NC8):
                        ci = v.tensor_tensor(obf[:, cc, :], osb[:, cc, :],
                                             gw[:, cc, :], ALU.mult)
                    ci.then_inc(vx, 1)
                c["vx"] += 2

                for oc in range(NC8):
                    b = oc % 2

                    @block.gpsimd
                    def _(g_, oc=oc, b=b):
                        g_.wait_ge(mm, wsl_free_mm[b])
                        g_.dma_start(out=wsl[:, b, :, :],
                                     in_=wo[:, :, 128 * oc:128 * oc + 128]
                                     .rearrange("c p o -> p c o")).then_inc(GL[b], 16)
                    c["gl%d" % b] += 16

                    @block.tensor
                    def _(t, oc=oc, b=b):
                        t.wait_ge(GL[b], c["gl%d" % b])
                        t.wait_ge(vx, c["vx"])
                        for cc in range(NC8):
                            mi = t.matmul(pb5[:, :], wsl[:, b, cc, :], obf[:, cc, :],
                                          start=(cc == 0), stop=(cc == NC8 - 1))
                            if cc == NC8 - 1:
                                mi.then_inc(mm, 1)
                    wsl_free_mm[b] = c["mm"] + 1
                    c["mm"] += 1

                    @block.vector
                    def _(v, oc=oc):
                        v.wait_ge(mm, c["mm"])
                        v.wait_ge(ld, c["ld"])
                        v.tensor_tensor(zpre[:, oc, :], pb5[:, :],
                                        xres[:, oc, :], ALU.add).then_inc(vx, 1)
                    c["vx"] += 1

                emit_ln_stripe(block, B, c, col0, z1_h, z1b_h)

        @block.sync
        def _(sy):
            sy.wait_ge(stS, c["stS"])
            sy.wait_ge(stG, c["stG"])



def emit_phase_b(nc, block, H, G, sems, P, c):
    z1_h, z1b_h = H["z1_fm"], H["z1_bf"]
    gw_h, gb_h = H["gate_w"], H["gate_b"]
    w1_h, b1_h = H["w1_t"], H["b1_c"]
    w2_h, b2_h = H["w2_t"], H["b2_r"]
    acc_h = H["acc_b"]
    z2_h, z2b_h = H["z2_fm"], H["z2_bf"]
    iden, ones_col, ones_rf, ones512, epsc, selsb = (
        G["iden"], G["ones_col"], G["ones_rf"], G["ones512"], G["epsc"],
        G["selsb"])

    import contextlib
    es = contextlib.ExitStack()
    def sb(name, shape, dtype):
        return es.enter_context(nc.sbuf_tensor(name + "_pb", shape, dtype))
    z1bf = sb("z1bf", [128, NC8, T], BF16)
    z1res = sb("z1res", [128, NC8, 512], F32)
    w1q = sb("w1q", [128, 2, NC8, 1024], BF16)
    w2q = sb("w2q", [128, 2, NC8, 1024], BF16)
    hbf = sb("hbf", [128, NC8, 512], BF16)
    ysb = sb("ysb", [128, 2, NC8, 512], F32)
    s8 = sb("s8", [8, NC8, 512], BF16)
    b1c_sb = sb("b1c_sb", [128, 8, 32], F32)
    sbc_sb = sb("sbc_sb", [128, 512], BF16)
    b2sb = sb("b2sb", [8, D], BF16)
    gwsb = sb("gwsb", [128, NC8, 8], F32)
    gbsb = sb("gbsb", [1, 8], F32)
    msb = sb("msb", [128, 1], F32)
    shsb = sb("shsb", [128, 8], F32)
    esb = sb("esb", [128, 8], F32)
    ssum = sb("ssum", [128, 1], F32)
    psb = sb("psb", [128, 1], F32)
    msk = sb("msk", [128, 8], F32)
    scb = sb("scb", [128, 8], BF16)
    zbf = hbf
    zsq = w1q[:, 0, :, 0:512]
    row_mean = z1res[0:1, 0, :]
    row_ms = z1res[0:1, 1, :]
    row_var = z1res[0:1, 2, :]
    row_r = z1res[0:1, 3, :]
    row_tmp = z1res[0:1, 4, :]
    pbG, pbT, pbB, pbH0, pbH1, pbY0, pbY1, pbX = P[:8]
    ld, mm, ev, vx = sems["ld"], sems["mm"], sems["ev"], sems["vx"]
    lr = sems["lr"]
    glA, glB, glC, glD, glE = (sems["glA"], sems["glB"], sems["glC"],
                               sems["glD"], sems["glE"])
    stS, stG, stH = sems["stS"], sems["stG"], sems["stH"]
    cst = sems["cst"]
    w2bf_h = H["w2_bf"]

    with es:
        W1S = [(glA, "glA"), (glB, "glB")]
        W2S = [(glC, "glC"), (glD, "glD")]
        pbTb = pbT.bitcast(BF16)
        NW = T // 512

        @block.sync
        def _(sy):
            sy.dma_start(out=z1bf[:, :, 0:2048],
                         in_=z1b_h[:, :, 0:2048]
                         .rearrange("c p t -> p c t")).then_inc(ld, 16)
            sy.dma_start(out=z1bf[:, :, 2048:4096],
                         in_=z1b_h[:, :, 2048:4096]
                         .rearrange("c p t -> p c t")).then_inc(ld, 16)
            sy.dma_start(out=gwsb[:, :, :],
                         in_=gw_h.rearrange("c p e -> p c e")).then_inc(ld, 16)
            sy.dma_start(out=gbsb[:, :], in_=gb_h).then_inc(ld, 16)
            sy.dma_start(out=b1c_sb[:, :, :], in_=b1_h).then_inc(ld, 16)
        c["ld"] += 80
        LD_STATIC = c["ld"]

        @block.gpsimd
        def _(g):
            g.dma_start(out=b2sb[:, :], in_=b2_h).then_inc(glE, 16)
        c["glE"] += 16

        B = dict(zpre=ysb[:, 0, :, :], zbf=zbf, zsq=zsq, row_mean=row_mean,
                 row_ms=row_ms,
                 row_var=row_var, row_r=row_r, row_tmp=row_tmp,
                 ones_col_bf=ones_col, ones_row_f=ones_rf, epsc=epsc,
                 pS=pbG, pQ=pbT, pR=pbB, pM=pbX, ld_static=LD_STATIC,
                 s_ld=ld, s_mm=mm, s_ev=ev, s_vx=vx, s_stS=stS, s_stG=stG)

        for w in range(NW):
            @block.sync
            def _(sy, w=w):
                if w >= 1:
                    sy.wait_ge(mm, c["mm"])
                sy.dma_start(out=z1res[:, :, :],
                             in_=z1_h[:, :, 512 * w:512 * w + 512]
                             .rearrange("c p t -> p c t")).then_inc(lr, 16)
            c["lr"] += 16

            for u in range(4):
                @block.tensor
                def _(t, w=w, u=u):
                    t.wait_ge(ld, LD_STATIC)
                    t.wait_ge(lr, c["lr"])
                    if u == 0 and w >= 1:
                        t.wait_ge(vx, c["vx"])
                    for cc in range(NC8):
                        t.matmul(pbG[:, 8 * u:8 * u + 8],
                                 z1res[:, cc, 128 * u:128 * u + 128],
                                 gwsb[:, cc, :],
                                 start=(cc == 0), stop=False)
                    t.matmul(pbG[:, 8 * u:8 * u + 8], ones_rf[:, :], gbsb[:, :],
                             start=False, stop=True).then_inc(mm, 1)
                c["mm"] += 1

                @block.vector
                def _(v, w=w, u=u):
                    v.wait_ge(mm, c["mm"])
                    v.reduce_max(msb[:, :], pbG[:, 8 * u:8 * u + 8],
                                 axis=AX.X).then_inc(vx, 1)
                    v.wait_ge(vx, c["vx"] + 1)
                    v.tensor_scalar(shsb[:, :], pbG[:, 8 * u:8 * u + 8],
                                    msb[:, :], None, ALU.subtract).then_inc(vx, 1)
                    v.wait_ge(vx, c["vx"] + 2)
                    v.tensor_scalar(msk[:, :], pbG[:, 8 * u:8 * u + 8],
                                    msb[:, :], None, ALU.is_ge).then_inc(vx, 1)
                c["vx"] += 3

                @block.scalar
                def _(sc, u=u):
                    sc.wait_ge(vx, c["vx"])
                    sc.activation(esb[:, :], shsb[:, :], AF.Exp).then_inc(ev, 1)
                c["ev"] += 1

                @block.vector
                def _(v, u=u):
                    v.wait_ge(ev, c["ev"])
                    v.reduce_sum(ssum[:, :], esb[:, :], axis=AX.X).then_inc(vx, 1)
                    v.wait_ge(vx, c["vx"] + 1)
                    v.reciprocal(psb[:, :], ssum[:, :]).then_inc(vx, 1)
                    v.wait_ge(vx, c["vx"] + 2)
                    v.tensor_scalar(scb[:, :], msk[:, :], psb[:, :], None,
                                    ALU.mult).then_inc(vx, 1)
                c["vx"] += 3

                @block.tensor
                def _(t, w=w, u=u):
                    t.wait_ge(vx, c["vx"])
                    if not (w == 0 and u == 0):
                        t.wait_ge(ev, c["ev"])
                    t.transpose(pbTb[0:8, 0:128], scb[:, :], iden[:, :]).then_inc(mm, 1)
                c["mm"] += 1

                @block.vector
                def _(v, w=w, u=u):
                    v.wait_ge(mm, c["mm"])
                    v.tensor_copy(s8[:, w, 128 * u:128 * u + 128],
                                  pbTb[0:8, 0:128]).then_inc(ev, 1)
                c["ev"] += 1

        w1_free_mm = [0, 0]
        w2_free_mm = [0, 0]
        STP = (stG, stH)
        STPN = ("stG", "stH")
        ysb_free = [0, 0]
        it = 0
        for e in range(8):
            for q in range(4):
                slot = q % 2

                if (e, q) == (0, 0):
                    @block.gpsimd
                    def _(g, e=e, q=q, slot=slot):
                        g.wait_ge(mm, w1_free_mm[slot])
                        g.dma_start(out=w1q[:, slot, :, :],
                                    in_=w1_h[e, :, :, 1024 * q:1024 * q + 1024]
                                    .rearrange("c p m -> p c m")
                                    ).then_inc(W1S[slot][0], 16)
                    c[W1S[slot][1]] += 16

                @block.sync
                def _(sy, e=e, q=q, slot=slot):
                    sy.wait_ge(mm, w2_free_mm[slot])
                    sy.wait_ge(cst, 16 * (16 * e + 4 * (q + 1)))
                    sy.dma_start(out=w2q[:, slot, :, :],
                                 in_=w2bf_h[e, 8 * q:8 * q + 8, :, :]
                                 .rearrange("m p d -> p m d")).then_inc(W2S[slot][0], 16)
                c[W2S[slot][1]] += 16

                en, qn = (e, q + 1) if q < 3 else (e + 1, 0)
                if en < 8:
                    slotn = qn % 2

                    @block.gpsimd
                    def _(g, en=en, qn=qn, slotn=slotn):
                        g.wait_ge(mm, w1_free_mm[slotn])
                        g.dma_start(out=w1q[:, slotn, :, :],
                                    in_=w1_h[en, :, :, 1024 * qn:1024 * qn + 1024]
                                    .rearrange("c p m -> p c m")
                                    ).then_inc(W1S[slotn][0], 16)
                    c[W1S[slotn][1]] += 16

                HB = (pbH0, pbH1, pbG, pbT)
                YB = (pbY0, pbY1, pbX)
                for w in range(NW):
                    par = it % 2
                    ev_need_h = c["ev"]

                    @block.tensor
                    def _(t, e=e, w=w):
                        t.wait_ge(ev, c["ev"])
                        t.wait_ge(vx, c["vx"])
                        t.matmul(pbB[:, :], selsb[:, e, :], s8[:, w, :],
                                 start=True, stop=True).then_inc(mm, 1)
                    c["mm"] += 1

                    @block.vector
                    def _(v):
                        v.wait_ge(mm, c["mm"])
                        v.tensor_copy(sbc_sb[:, :], pbB[:, :]).then_inc(ev, 1)
                    c["ev"] += 1

                    hev_base = c["ev"]
                    for mc in range(NC8):
                        pbh = HB[mc % 4]

                        @block.tensor
                        def _(t, e=e, q=q, w=w, mc=mc, slot=slot, pbh=pbh,
                              hev_base=hev_base, ev_need_h=ev_need_h):
                            t.wait_ge(W1S[slot][0], c[W1S[slot][1]])
                            t.wait_ge(ld, LD_STATIC)
                            if mc == 0:
                                t.wait_ge(mm, c["mm"] - 1)
                            if mc < 4:
                                t.wait_ge(ev, ev_need_h)
                            else:
                                t.wait_ge(ev, hev_base + (mc - 4) + 1)
                            for cc in range(NC8):
                                mi = t.matmul(pbh[:, :],
                                              w1q[:, slot, cc, 128 * mc:128 * mc + 128],
                                              z1bf[:, cc, 512 * w:512 * w + 512],
                                              start=(cc == 0), stop=(cc == NC8 - 1))
                            mi.then_inc(mm, 1)
                        c["mm"] += 1

                        @block.vector
                        def _(v, e=e, q=q, mc=mc, pbh=pbh):
                            v.wait_ge(mm, c["mm"])
                            v.tensor_scalar(hbf[:, mc, :], pbh[:, :],
                                            b1c_sb[:, e, 8 * q + mc:8 * q + mc + 1],
                                            0.0, ALU.add, ALU.max).then_inc(ev, 1)
                        c["ev"] += 1

                    if e == 0 and (q == 0 or q == 3):
                        @block.sync
                        def _(sy, w=w):
                            sy.wait_ge(vx, c["vx"])
                            sy.dma_start(out=z1res[:, :, :],
                                         in_=z1_h[:, :, 512 * w:512 * w + 512]
                                         .rearrange("c p t -> p c t")
                                         ).then_inc(lr, 16)
                        c["lr"] += 16

                    y_vx_base = c["vx"]
                    y_vx_after = []
                    for dc in range(NC8):
                        pby = YB[dc % 3]
                        vx_need = y_vx_base if dc < 3 else y_vx_after[dc - 3]

                        @block.tensor
                        def _(t, e=e, q=q, w=w, dc=dc, slot=slot, pby=pby,
                              vx_need=vx_need):
                            t.wait_ge(W2S[slot][0], c[W2S[slot][1]])
                            t.wait_ge(ev, c["ev"])
                            t.wait_ge(vx, vx_need)
                            for mc in range(NC8):
                                mi = t.matmul(pby[:, :],
                                              w2q[:, slot, mc, 128 * dc:128 * dc + 128],
                                              hbf[:, mc, :],
                                              start=(mc == 0), stop=(mc == NC8 - 1))
                            mi.then_inc(mm, 1)
                        c["mm"] += 1

                        @block.vector
                        def _(v, e=e, q=q, w=w, dc=dc, pby=pby, par=par):
                            v.wait_ge(mm, c["mm"])
                            v.wait_ge(ev, c["ev"])
                            v.wait_ge(STP[par], ysb_free[par])
                            if e == 0 and q == 0:
                                v.wait_ge(lr, c["lr"])
                                v.tensor_tensor(ysb[:, par, dc, :], pby[:, :],
                                                sbc_sb[:, :], ALU.mult).then_inc(vx, 1)
                                v.wait_ge(vx, c["vx"] + 1)
                                v.tensor_tensor(ysb[:, par, dc, :], ysb[:, par, dc, :],
                                                z1res[:, dc, :], ALU.add).then_inc(vx, 1)
                            else:
                                v.tensor_tensor(ysb[:, par, dc, :], pby[:, :],
                                                sbc_sb[:, :], ALU.mult).then_inc(vx, 1)
                        c["vx"] += 2 if (e == 0 and q == 0) else 1
                        y_vx_after.append(c["vx"])

                    if e == 0 and q == 3:
                        for dc in range(NC8):
                            pby = YB[dc % 3]

                            @block.tensor
                            def _(t, w=w, dc=dc, pby=pby):
                                t.wait_ge(glE, c["glE"])
                                t.wait_ge(vx, c["vx"])
                                t.matmul(pby[:, :], b2sb[:, 128 * dc:128 * dc + 128],
                                         s8[:, w, :], start=True,
                                         stop=True).then_inc(mm, 1)
                            c["mm"] += 1

                            @block.vector
                            def _(v, w=w, dc=dc, pby=pby, par=par):
                                v.wait_ge(mm, c["mm"])
                                v.wait_ge(lr, c["lr"])
                                v.tensor_tensor(ysb[:, par, dc, :], ysb[:, par, dc, :],
                                                pby[:, :], ALU.add).then_inc(vx, 1)
                            c["vx"] += 1

                    @block.gpsimd
                    def _(g, e=e, q=q, w=w, par=par):
                        g.wait_ge(vx, c["vx"])
                        if e == 0 and q == 0:
                            g.dma_start(out=acc_h[:, :, 512 * w:512 * w + 512]
                                        .rearrange("c p t -> p c t"),
                                        in_=ysb[:, par, :, :]).then_inc(STP[par], 16)
                        else:
                            g.dma_start(out=acc_h[:, :, 512 * w:512 * w + 512]
                                        .rearrange("c p t -> p c t"),
                                        in_=ysb[:, par, :, :],
                                        accum_op=ALU.add).then_inc(STP[par], 16)
                    c[STPN[par]] += 16
                    ysb_free[par] = c[STPN[par]]
                    it += 1

                w1_free_mm[slot] = c["mm"]
                w2_free_mm[slot] = c["mm"]

        for w in range(NW):
            @block.sync
            def _(sy, w=w):
                sy.wait_ge(stG, c["stG"])
                sy.wait_ge(stH, c["stH"])
                if w >= 1:
                    sy.wait_ge(stS, c["stS"])
                    sy.wait_ge(mm, c["mm"])
                sy.dma_start(out=ysb[:, 0, :, :],
                             in_=acc_h[:, :, 512 * w:512 * w + 512]
                             .rearrange("c p t -> p c t")).then_inc(lr, 16)
            c["lr"] += 16
            B["zpre_ready"] = (lr, c["lr"])
            emit_ln_stripe(block, B, c, 512 * w, z2_h, z2b_h)

        @block.sync
        def _(sy):
            sy.wait_ge(stS, c["stS"])
            sy.wait_ge(stG, c["stG"])



SC1 = float(np.float32(1.0) / np.sqrt(np.float32(16 + 512 + 512)))


def emit_phase_c(nc, block, H, G, sems, P, c):
    z2_h, z2b_h = H["z2_fm"], H["z2_bf"]
    wdq_h, bdq_h = H["wdq_t"], H["bdq_c"]
    wuq_h, buq_h = H["wuq_t"], H["buq_c"]
    wqr_h, bqr_h = H["wqr_t"], H["bqr_c"]
    wkr_h, bkr_h = H["wkr_t"], H["bkr_c"]
    wdkv_h, bdkv_h = H["wdkv_t"], H["bdkv_c"]
    wuk_h, buk_h = H["wuk_t"], H["buk_c"]
    wuv_h, buv_h = H["wuv_t"], H["buv_r"]
    wo1_h, wob_h = H["wo1_t"], H["wob_c"]
    rc_h, rs_h = H["l1_ropec"], H["l1_ropes"]
    z3_h, z3b_h = H["z3_fm"], H["z3_bf"]
    iden, ones_col, ones_rb, ones_rf, epsc = (G["iden"], G["ones_col"],
                                              G["ones_rb"], G["ones_rf"],
                                              G["epsc"])

    import contextlib
    es = contextlib.ExitStack()
    def sb(name, shape, dtype):
        return es.enter_context(nc.sbuf_tensor(name + "_pc", shape, dtype))
    z2bf = sb("z2bf", [128, NC8, S], BF16)
    ctq = sb("ctq", [128, 4, S], BF16)
    ctkv = sb("ctkv", [128, 4, S], BF16)
    kbf = sb("kbf", [128, NC8, S], BF16)
    vaug = sb("vaug", [128, 16, 16, 65], BF16)
    wsl = sb("wsl", [128, 2, NC8, 128], BF16)
    wuvsb = sb("wuvsb", [128, 4, D], BF16)
    buvsb = sb("buvsb", [1, D], BF16)
    tbl2C = sb("tbl2C", [128, 2, 512], F32)
    tbl2S = sb("tbl2S", [128, 2, 512], F32)
    rstgA = sb("rstgA", [128, 512], BF16)
    rstgB = sb("rstgB", [128, 512], BF16)
    swstg = sb("swstg", [128, 512], BF16)
    pstg = sb("pstg", [128, 512], BF16)
    ktc_st = sb("ktc_st", [128, 512], BF16)
    numst = sb("numst", [64, 2, 512], F32)
    denr = sb("denr", [1, 2, 512], F32)
    denri = sb("denri", [1, 2, 512], F32)
    obf = sb("obf", [128, NC8, 512], BF16)
    bdq = sb("bdq", [128, 4], F32)
    buq = sb("buq", [128, NC8], F32)
    bqr = sb("bqr", [128, NC8], F32)
    bkr = sb("bkr", [128, NC8], F32)
    bdkv = sb("bdkv", [128, 4], F32)
    buk = sb("buk", [128, NC8], F32)
    wob = sb("wob", [128, NC8], F32)
    zpre = sb("zpre", [128, NC8, 512], F32)
    zsq = sb("zsq", [128, NC8, 512], BF16)
    ctkvF = ctkv.bitcast(F32)
    row_mean = ctkvF[0:1, 0, 0:512]
    row_ms = ctkvF[0:1, 0, 512:1024]
    row_var = ctkvF[0:1, 1, 0:512]
    row_r = ctkvF[0:1, 1, 512:1024]
    row_tmp = ctkvF[0:1, 2, 0:512]
    pb0, pb1, pb2, pb3, pb4, pb5, pb6, pb7 = P[:8]
    ld, lr, mm, ev, vx = (sems["ld"], sems["lr"], sems["mm"], sems["ev"],
                          sems["vx"])
    glx, gl0, gl1, glt = (sems["glx"], sems["gl0"], sems["gl1"], sems["glt"])
    stS, stG = sems["stS"], sems["stG"]

    with es:
        GL = (gl0, gl1)
        GLN = ("gl0", "gl1")

        PS4a, PS4b = P[8], P[9]
        def E_ap(kt):
            return z2bf[:, (512 * kt) // S, (512 * kt) % S:(512 * kt) % S + 512]

        def E2_ap(jp):
            return z2bf[:, (1024 * jp) // S,
                        (1024 * jp) % S:(1024 * jp) % S + 1024]

        def q_ap(oc, p0=0, pn=128):
            base = 8192 + 512 * oc
            return z2bf[p0:p0 + pn, base // S, base % S:base % S + 512]

        @block.sync
        def _(sy):
            sy.dma_start(out=bdq[:, :], in_=bdq_h).then_inc(ld, 16)
            sy.dma_start(out=buq[:, :], in_=buq_h).then_inc(ld, 16)
            sy.dma_start(out=bqr[:, :], in_=bqr_h).then_inc(ld, 16)
            sy.dma_start(out=bkr[:, :], in_=bkr_h).then_inc(ld, 16)
            sy.dma_start(out=bdkv[:, :], in_=bdkv_h).then_inc(ld, 16)
            sy.dma_start(out=buk[:, :], in_=buk_h).then_inc(ld, 16)
            sy.dma_start(out=wob[:, :], in_=wob_h).then_inc(ld, 16)
        c["ld"] += 112
        LD_STATIC = c["ld"]

        @block.gpsimd
        def _(g):
            g.dma_start(out=wuvsb[:, :, :],
                        in_=wuv_h.rearrange("c p o -> p c o")).then_inc(glt, 16)
            g.dma_start(out=buvsb[:, :], in_=buv_h).then_inc(glt, 16)
        c["glt"] += 32
        GLT_C = c["glt"]

        @block.vector
        def _(v):
            for j in range(16):
                mi = v.memset(vaug[:, j, :, 64:65], 1.0)
            mi.then_inc(vx, 1)
        c["vx"] += 1
        VXC = c["vx"]

        B = dict(zpre=zpre, zbf=obf, zsq=zsq, row_mean=row_mean, row_ms=row_ms,
                 row_var=row_var, row_r=row_r, row_tmp=row_tmp,
                 ones_col_bf=ones_col, ones_row_f=ones_rf, epsc=epsc,
                 pS=pb6, pQ=pb7, pR=pb4, pM=pb5, ld_static=LD_STATIC,
                 s_ld=ld, s_mm=mm, s_ev=ev, s_vx=vx, s_stS=stS, s_stG=stG)

        wsl_free_mm = [0, 0]

        def load_wslice(src_ap, oc, b, nchunk=NC8):
            @block.gpsimd
            def _(g, src_ap=src_ap, oc=oc, b=b, nchunk=nchunk):
                g.wait_ge(mm, wsl_free_mm[b])
                g.dma_start(out=wsl[:, b, 0:nchunk, :],
                            in_=src_ap[:, :, 128 * oc:128 * oc + 128]
                            .rearrange("c p o -> p c o")).then_inc(GL[b], 16)
            c[GLN[b]] += 16

        TBL = {"i": 0, "ri": 0, "lr": [], "free": [], "sched": []}

        def tbl_issue():
            i = TBL["i"]
            if i >= len(TBL["sched"]):
                return
            oc, w = TBL["sched"][i]
            sl = i % 2
            need = TBL["free"][i - 2] if i >= 2 else 0

            @block.sync
            def _(sy, oc=oc, w=w, sl=sl, need=need):
                sy.wait_ge(vx, need)
                sy.dma_start(out=tbl2C[:, sl, :],
                             in_=rc_h[oc, :, 512 * w:512 * w + 512]).then_inc(lr, 16)
                sy.dma_start(out=tbl2S[:, sl, :],
                             in_=rs_h[oc, :, 512 * w:512 * w + 512]).then_inc(lr, 16)
            c["lr"] += 32
            TBL["lr"].append(c["lr"])
            TBL["i"] += 1

        def rope_slice(srcpre, dst_ap, addsrc):
            ri = TBL["ri"]
            sl = ri % 2
            lr_need = TBL["lr"][ri]

            @block.vector
            def _(v, sl=sl, lr_need=lr_need):
                v.wait_ge(ev, c["ev"])
                v.wait_ge(vx, c["vx"])
                v.wait_ge(lr, lr_need)
                v.tensor_tensor(rstgA[:, :], srcpre, tbl2C[:, sl, :],
                                ALU.mult).then_inc(vx, 1)
                v.tensor_tensor(rstgB[0:32, :], srcpre[32:64, :],
                                tbl2S[32:64, sl, :], ALU.mult).then_inc(vx, 1)
                v.tensor_tensor(rstgB[32:64, :], srcpre[0:32, :],
                                tbl2S[0:32, sl, :], ALU.mult).then_inc(vx, 1)
                v.tensor_tensor(rstgB[64:96, :], srcpre[96:128, :],
                                tbl2S[96:128, sl, :], ALU.mult).then_inc(vx, 1)
                v.tensor_tensor(rstgB[96:128, :], srcpre[64:96, :],
                                tbl2S[64:96, sl, :], ALU.mult).then_inc(vx, 1)
                v.wait_ge(vx, c["vx"] + 5)
                if addsrc is None:
                    v.tensor_tensor(dst_ap, rstgA[:, :], rstgB[:, :],
                                    ALU.add).then_inc(vx, 1)
                else:
                    v.tensor_tensor(rstgA[:, :], rstgA[:, :], rstgB[:, :],
                                    ALU.add).then_inc(vx, 1)
                    v.wait_ge(vx, c["vx"] + 6)
                    v.tensor_tensor(dst_ap, rstgA[:, :], addsrc,
                                    ALU.add).then_inc(vx, 1)
            TBL["free"].append(c["vx"] + 5)
            c["vx"] += 6 if addsrc is None else 7
            TBL["ri"] += 1
            tbl_issue()

        for s in range(NSEQ):
            @block.gpsimd
            def _(g, s=s):
                if s == 1:
                    g.wait_ge(mm, c["mm"])
                g.dma_start(out=z2bf[:, :, :],
                            in_=z2b_h[:, :, s * S:(s + 1) * S]
                            .rearrange("c p t -> p c t")).then_inc(glx, 16)
            c["glx"] += 16

            TBL["sched"].extend([(oc, w) for oc in range(NC8) for w in range(4)])
            TBL["sched"].extend([(oc, w) for w in range(4) for oc in range(NC8)])
            while TBL["i"] < TBL["ri"] + 2:
                tbl_issue()

            for (wsrc, dst, bias, no) in ((wdq_h, ctq, bdq, 4), (wdkv_h, ctkv, bdkv, 4)):
                for oc in range(no):
                    b = oc % 2
                    load_wslice(wsrc, oc, b)
                    ev_base = c["ev"]

                    @block.tensor
                    def _(t, oc=oc, b=b, ev_base=ev_base):
                        t.wait_ge(GL[b], c[GLN[b]])
                        t.wait_ge(glx, c["glx"])
                        t.wait_ge(ev, ev_base)
                        for w in range(4):
                            pband = pb0 if (w % 2 == 0) else pb1
                            if w >= 2:
                                t.wait_ge(ev, ev_base + 1)
                            for cc in range(NC8):
                                mi = t.matmul(pband[:, :], wsl[:, b, cc, :],
                                              z2bf[:, cc, 512 * w:512 * w + 512],
                                              start=(cc == 0), stop=(cc == NC8 - 1))
                            mi.then_inc(mm, 1)
                    wsl_free_mm[b] = c["mm"] + 4
                    c["mm"] += 4

                    @block.vector
                    def _(v, oc=oc, dst=dst, bias=bias):
                        for j in range(2):
                            v.wait_ge(mm, c["mm"] - 2 + 2 * j)
                            v.wait_ge(ld, LD_STATIC)
                            v.tensor_scalar(dst[:, oc, 1024 * j:1024 * j + 1024],
                                            P[8][:, 0:1024], bias[:, oc:oc + 1],
                                            None, ALU.add).then_inc(ev, 1)
                    c["ev"] += 2

            for oc in range(NC8):
                b = oc % 2
                load_wslice(wkr_h, oc, b)
                ev_base = c["ev"]

                @block.tensor
                def _(t, oc=oc, b=b, ev_base=ev_base):
                    t.wait_ge(GL[b], c[GLN[b]])
                    t.wait_ge(glx, c["glx"])
                    t.wait_ge(ev, ev_base)
                    for w in range(4):
                        pband = pb0 if (w % 2 == 0) else pb1
                        if w >= 2:
                            t.wait_ge(ev, ev_base + 1)
                        for cc in range(NC8):
                            mi = t.matmul(pband[:, :], wsl[:, b, cc, :],
                                          z2bf[:, cc, 512 * w:512 * w + 512],
                                          start=(cc == 0), stop=(cc == NC8 - 1))
                        mi.then_inc(mm, 1)
                wsl_free_mm[b] = c["mm"] + 4
                c["mm"] += 4

                @block.vector
                def _(v, oc=oc):
                    for j in range(2):
                        v.wait_ge(mm, c["mm"] - 2 + 2 * j)
                        v.wait_ge(ld, LD_STATIC)
                        v.tensor_scalar(kbf[:, oc, 1024 * j:1024 * j + 1024],
                                        P[8][:, 0:1024], bkr[:, oc:oc + 1],
                                        None, ALU.add).then_inc(ev, 1)
                c["ev"] += 2

            for oc in range(NC8):
                b = oc % 2
                load_wslice(wuk_h, oc, b, nchunk=4)
                for w in range(4):
                    @block.tensor
                    def _(t, oc=oc, b=b, w=w):
                        t.wait_ge(GL[b], c[GLN[b]])
                        t.wait_ge(ev, c["ev"])
                        for cc in range(4):
                            mi = t.matmul(pb2[:, :], wsl[:, b, cc, :],
                                          ctkv[:, cc, 512 * w:512 * w + 512],
                                          start=(cc == 0), stop=(cc == 3))
                        mi.then_inc(mm, 1)
                    c["mm"] += 1

                    @block.vector
                    def _(v, oc=oc, w=w):
                        v.wait_ge(mm, c["mm"])
                        v.wait_ge(vx, c["vx"])
                        v.tensor_scalar(ktc_st[:, :], pb2[:, :],
                                        buk[:, oc:oc + 1], None,
                                        ALU.add).then_inc(ev, 1)
                        v.tensor_copy(pstg[:, :],
                                      kbf[:, oc, 512 * w:512 * w + 512]).then_inc(ev, 1)
                    c["ev"] += 2

                    rope_slice(pstg[:, :],
                               kbf[:, oc, 512 * w:512 * w + 512], ktc_st[:, :])
                wsl_free_mm[b] = c["mm"]

            for j in range(16):
                for n2 in range(2):
                    pband = pb0 if (n2 == 0) else pb1

                    @block.tensor
                    def _(t, j=j, n2=n2, pband=pband, s=s):
                        t.wait_ge(glt, GLT_C)
                        t.wait_ge(ev, c["ev"])
                        if s == 0 and j == 0 and n2 == 0:
                            t.wait_ge(vx, VXC)
                        for cc in range(4):
                            t.matmul(pband[:, :], ctkv[:, cc, 128 * j:128 * j + 128],
                                     wuvsb[:, cc, 512 * n2:512 * n2 + 512],
                                     start=(cc == 0), stop=False)
                        t.matmul(pband[:, :], ones_rb[:, :],
                                 buvsb[0:1, 512 * n2:512 * n2 + 512],
                                 start=False, stop=True).then_inc(mm, 1)
                    c["mm"] += 1

                    @block.vector
                    def _(v, j=j, n2=n2, pband=pband):
                        v.wait_ge(mm, c["mm"])
                        v.tensor_copy(vaug[:, j, 8 * n2:8 * n2 + 8, 0:64],
                                      pband[:, :].rearrange(
                                          "p (a b) -> p a b", a=8)).then_inc(ev, 1)
                    c["ev"] += 1

            for w in range(4):
                col0 = s * S + 512 * w

                for oc in range(NC8):
                    b = oc % 2
                    load_wslice(wuq_h, oc, b, nchunk=4)

                    @block.tensor
                    def _(t, oc=oc, b=b, w=w):
                        t.wait_ge(GL[b], c[GLN[b]])
                        t.wait_ge(ev, c["ev"])
                        t.wait_ge(vx, c["vx"])
                        for cc in range(4):
                            mi = t.matmul(pb2[:, :], wsl[:, b, cc, :],
                                          ctq[:, cc, 512 * w:512 * w + 512],
                                          start=(cc == 0), stop=(cc == 3))
                        mi.then_inc(mm, 1)
                    wsl_free_mm[b] = c["mm"] + 1
                    c["mm"] += 1

                    @block.vector
                    def _(v, oc=oc):
                        v.wait_ge(mm, c["mm"])
                        v.wait_ge(vx, c["vx"])
                        v.tensor_scalar(ktc_st[:, :], pb2[:, :],
                                        buq[:, oc:oc + 1], None,
                                        ALU.add).then_inc(ev, 1)
                    c["ev"] += 1

                    b2_ = (oc + 1) % 2
                    load_wslice(wqr_h, oc, b2_, nchunk=4)

                    @block.tensor
                    def _(t, oc=oc, b2_=b2_, w=w):
                        t.wait_ge(GL[b2_], c[GLN[b2_]])
                        t.wait_ge(ev, c["ev"])
                        for cc in range(4):
                            mi = t.matmul(pb3[:, :], wsl[:, b2_, cc, :],
                                          ctq[:, cc, 512 * w:512 * w + 512],
                                          start=(cc == 0), stop=(cc == 3))
                        mi.then_inc(mm, 1)
                    wsl_free_mm[b2_] = c["mm"] + 1
                    c["mm"] += 1

                    @block.vector
                    def _(v, oc=oc):
                        v.wait_ge(mm, c["mm"])
                        v.wait_ge(vx, c["vx"])
                        v.tensor_scalar(pstg[:, :], pb3[:, :],
                                        bqr[:, oc:oc + 1], None,
                                        ALU.add).then_inc(ev, 1)
                    c["ev"] += 1

                    rope_slice(pstg[:, :], q_ap(oc), ktc_st[:, :])

                PAIRS = (PS4a[:, 0:1024], PS4a[:, 1024:2048],
                         PS4b[:, 0:1024])
                VX_Q = c["vx"]
                norm_pending = []

                def emit_score_norm(c=c):
                    h, par, vx_recip = norm_pending.pop(0)
                    p0_ = 64 * (h % 2)
                    hc_ = h // 2

                    @block.tensor
                    def _(t, par=par, vx_recip=vx_recip):
                        t.wait_ge(vx, vx_recip)
                        t.matmul(pb7[0:64, :], ones_rf[:, 0:64], denri[:, par, :],
                                 start=True, stop=True).then_inc(mm, 1)
                    c["mm"] += 1

                    @block.vector
                    def _(v, par=par, p0_=p0_, hc_=hc_):
                        v.wait_ge(mm, c["mm"])
                        v.tensor_tensor(obf[p0_:p0_ + 64, hc_, :], numst[:, par, :],
                                        pb7[0:64, :], ALU.mult).then_inc(vx, 1)
                    c["vx"] += 1

                copies_done = c["vx"]
                for h in range(16):
                    p0 = 64 * (h % 2)
                    hc = h // 2
                    par = h % 2
                    ev_base = c["ev"]
                    for jp in range(8):
                        pair = PAIRS[jp % 3]

                        @block.tensor
                        def _(t, h=h, jp=jp, w=w, pair=pair, p0=p0, hc=hc,
                              ev_base=ev_base):
                            t.wait_ge(vx, VX_Q)
                            t.wait_ge(ev, max(0, ev_base + jp - 2))
                            for half in range(2):
                                kt = 2 * jp + half
                                mi = t.matmul(pair[:, 512 * half:512 * half + 512],
                                              kbf[p0:p0 + 64, hc,
                                                  128 * kt:128 * kt + 128],
                                              q_ap(2 * hc + (h % 2), p0, 64),
                                              start=True, stop=True)
                            mi.then_inc(mm, 1)
                        c["mm"] += 1

                        @block.scalar
                        def _(sc, jp=jp, pair=pair):
                            sc.wait_ge(mm, c["mm"])
                            sc.activation(E2_ap(jp), pair[:, :], AF.Exp,
                                          scale=SC1).then_inc(ev, 1)
                        c["ev"] += 1

                    if norm_pending:
                        emit_score_norm()

                    @block.tensor
                    def _(t, h=h, w=w, copies_done=copies_done, ev_base=ev_base):
                        t.wait_ge(ev, ev_base + 4)
                        t.wait_ge(vx, copies_done)
                        for kt in range(8):
                            mi = t.matmul(pb6[0:65, :], vaug[:, kt, h, :],
                                          E_ap(kt), start=(kt == 0), stop=False)
                        mi.then_inc(mm, 1)
                    c["mm"] += 1

                    @block.tensor
                    def _(t, h=h, w=w):
                        t.wait_ge(ev, c["ev"])
                        for kt in range(8, 16):
                            mi = t.matmul(pb6[0:65, :], vaug[:, kt, h, :],
                                          E_ap(kt), start=False, stop=(kt == 15))
                        mi.then_inc(mm, 1)
                    c["mm"] += 1

                    @block.vector
                    def _(v, h=h, par=par):
                        v.wait_ge(mm, c["mm"])
                        v.tensor_copy(numst[:, par, :], pb6[0:64, :]).then_inc(vx, 1)
                        v.tensor_copy(denr[:, par, :], pb6[64:65, :]).then_inc(vx, 1)
                        v.wait_ge(vx, c["vx"] + 2)
                        v.reciprocal(denri[:, par, :],
                                     denr[:, par, :]).then_inc(vx, 1)
                    copies_done = c["vx"] + 2
                    c["vx"] += 3
                    norm_pending.append((h, par, c["vx"]))
                emit_score_norm()

                @block.sync
                def _(sy, w=w, col0=col0):
                    sy.wait_ge(vx, c["vx"])
                    sy.wait_ge(stS, c["stS"])
                    sy.wait_ge(stG, c["stG"])
                    sy.dma_start(out=zpre[:, :, :],
                                 in_=z2_h[:, :, col0:col0 + 512]
                                 .rearrange("c p t -> p c t")).then_inc(lr, 16)
                c["lr"] += 16

                for oc in range(NC8):
                    b = oc % 2
                    load_wslice(wo1_h, oc, b)

                    @block.tensor
                    def _(t, oc=oc, b=b):
                        t.wait_ge(GL[b], c[GLN[b]])
                        t.wait_ge(vx, c["vx"])
                        t.wait_ge(ev, c["ev"])
                        for cc in range(NC8):
                            mi = t.matmul(pb2[:, :], wsl[:, b, cc, :], obf[:, cc, :],
                                          start=(cc == 0), stop=(cc == NC8 - 1))
                        mi.then_inc(mm, 1)
                    wsl_free_mm[b] = c["mm"] + 1
                    c["mm"] += 1

                    @block.vector
                    def _(v, oc=oc):
                        v.wait_ge(mm, c["mm"])
                        v.wait_ge(lr, c["lr"])
                        v.scalar_tensor_tensor(zpre[:, oc, :], pb2[:, :],
                                               wob[:, oc:oc + 1], zpre[:, oc, :],
                                               ALU.add, ALU.add).then_inc(vx, 1)
                    c["vx"] += 1

                emit_ln_stripe(block, B, c, col0, z3_h, z3b_h)

        @block.sync
        def _(sy):
            sy.wait_ge(stS, c["stS"])
            sy.wait_ge(stG, c["stG"])



def emit_phase_d(nc, block, H, G, sems, P, c):
    z3_h, z3b_h = H["z3_fm"], H["z3_bf"]
    rm_h, rup_h, wc_h = H["rmT_bd"], H["rupT_bd"], H["wc_t"]
    h_hbm, acc_h, out_h = H["h_d"], H["acc_d"], H["out_fm"]
    ones_col, ones_rf, epsc = G["ones_col"], G["ones_rf"], G["epsc"]

    import contextlib
    es = contextlib.ExitStack()
    def sb(name, shape, dtype):
        return es.enter_context(nc.sbuf_tensor(name + "_pd", shape, dtype))
    G_sb = sb("G_sb", [128, 4, NC8, 512], BF16)
    hst = sb("hst", [128, 32, 512], BF16)
    z3w = sb("z3w", [128, 2, 4096], BF16)
    wce = sb("wce", [128, 16, D], BF16)
    he = sb("he", [128, 2, 16, 512], BF16)
    ysb = sb("ysb", [128, 2, NC8, 512], F32)
    rmsl = sb("rmsl", [128, 2, 128], BF16)
    rupsl = sb("rupsl", [128, 2, 512], BF16)
    wceF = wce.bitcast(F32)
    zpre = wceF[:, 0:NC8, :]
    zbf = he[:, 0, 0:NC8, :]
    zsq = he[:, 1, 0:NC8, :]
    z3wF = z3w.bitcast(F32)
    row_mean = z3wF[0:1, 0, 0:512]
    row_ms = z3wF[0:1, 0, 512:1024]
    row_var = z3wF[0:1, 0, 1024:1536]
    row_r = z3wF[0:1, 1, 0:512]
    row_tmp = z3wF[0:1, 1, 512:1024]
    pb0, pb1, pb2, pb3, pb4, pb5, pb6, pb7 = P[:8]
    PS4 = (P[8], P[9])
    ld, lr, mm, ev, vx = (sems["ld"], sems["lr"], sems["mm"], sems["ev"],
                          sems["vx"])
    glA, glB = sems["glA"], sems["glB"]
    stS, stG, stH = sems["stS"], sems["stG"], sems["stH"]

    with es:
        LD_STATIC = G["LD0"]

        B = dict(zpre=zpre, zbf=zbf, zsq=zsq, row_mean=row_mean, row_ms=row_ms,
                 row_var=row_var, row_r=row_r, row_tmp=row_tmp,
                 ones_col_bf=ones_col, ones_row_f=ones_rf, epsc=epsc,
                 pS=pb6, pQ=pb7, pR=pb4, pM=pb5, ld_static=LD_STATIC,
                 s_ld=ld, s_mm=mm, s_ev=ev, s_vx=vx, s_stS=stS, s_stG=stG)

        for h4 in range(4):
            for cc in range(NC8):
                sl = (h4 * NC8 + cc) % 2

                @block.gpsimd
                def _(g, h4=h4, cc=cc, sl=sl):
                    g.wait_ge(mm, max(0, c["mm"] - 1))
                    g.dma_start(out=rmsl[:, sl, :], in_=rm_h[h4, cc, :, :]).then_inc(glA, 16)
                    g.dma_start(out=rupsl[:, sl, :], in_=rup_h[h4, cc, :, :]).then_inc(glA, 16)
                c["glA"] += 32

                @block.tensor
                def _(t, h4=h4, cc=cc, sl=sl):
                    t.wait_ge(glA, c["glA"])
                    t.wait_ge(ev, c["ev"])
                    t.matmul(pb0[:, :], rmsl[:, sl, :], rupsl[:, sl, :],
                             start=True, stop=True).then_inc(mm, 1)
                c["mm"] += 1

                @block.vector
                def _(v, h4=h4, cc=cc):
                    v.wait_ge(mm, c["mm"])
                    v.tensor_copy(G_sb[:, h4, cc, :], pb0[:, :]).then_inc(ev, 1)
                c["ev"] += 1

        NW = T // 512
        for w in range(NW):
            zsl = w % 2

            @block.sync
            def _(sy, w=w, zsl=zsl):
                sy.wait_ge(mm, c["mm"] - 32)
                sy.dma_start(out=z3w[:, zsl, :].rearrange("p (c t) -> p c t", c=NC8),
                             in_=z3b_h[:, :, 512 * w:512 * w + 512]
                             .rearrange("c p t -> p c t")).then_inc(lr, 16)
            c["lr"] += 16

            for h4 in range(4):
                ev_base = c["ev"]
                for cc in range(NC8):
                    ps = PS4[cc % 2]

                    @block.tensor
                    def _(t, w=w, h4=h4, cc=cc, ps=ps, ev_base=ev_base, zsl=zsl):
                        t.wait_ge(lr, c["lr"])
                        if cc < 2:
                            t.wait_ge(ev, ev_base)
                        else:
                            t.wait_ge(ev, ev_base + (cc - 2) + 1)
                        for sub in range(4):
                            mi = t.matmul(ps[:, 512 * sub:512 * sub + 512],
                                          G_sb[:, h4, cc, 128 * sub:128 * sub + 128],
                                          z3w[:, zsl, 512 * cc:512 * cc + 512],
                                          start=True, stop=True)
                        mi.then_inc(mm, 1)
                    c["mm"] += 1

                    @block.scalar
                    def _(sc, cc=cc, ps=ps):
                        sc.wait_ge(mm, c["mm"])
                        if cc == 0:
                            sc.wait_ge(stG, c["stG"])
                        sc.activation(hst[:, 4 * cc:4 * cc + 4, :],
                                      ps[:, :].rearrange("p (a b) -> p a b", a=4),
                                      AF.Gelu).then_inc(ev, 1)
                    c["ev"] += 1

                @block.gpsimd
                def _(g, w=w, h4=h4):
                    g.wait_ge(ev, c["ev"])
                    g.dma_start(out=h_hbm[32 * h4:32 * h4 + 32, :, 512 * w:512 * w + 512]
                                .rearrange("m p t -> p m t"),
                                in_=hst[:, :, :]).then_inc(stG, 16)
                c["stG"] += 16

        D1_STG = c["stG"]
        STP = (stG, stH)
        STPN = ("stG", "stH")
        acc_free = [c["stG"], c["stH"]]
        he_free_mm = [0, 0]
        wce_free_mm = 0
        it = 0
        for ei in range(8):
            @block.gpsimd
            def _(g, ei=ei):
                g.wait_ge(mm, wce_free_mm)
                g.dma_start(out=wce[:, :, :],
                            in_=wc_h[16 * ei:16 * ei + 16, :, :]
                            .rearrange("m p d -> p m d")).then_inc(glB, 16)
            c["glB"] += 16

            for w in range(NW):
                par = it % 2

                @block.sync
                def _(sy, ei=ei, w=w, par=par):
                    sy.wait_ge(mm, he_free_mm[par])
                    sy.wait_ge(stG, D1_STG)
                    sy.dma_start(out=he[:, par, :, :],
                                 in_=h_hbm[16 * ei:16 * ei + 16, :, 512 * w:512 * w + 512]
                                 .rearrange("m p t -> p m t")).then_inc(lr, 16)
                c["lr"] += 16

                if ei == 0:
                    @block.sync
                    def _(sy, w=w, par=par):
                        sy.wait_ge(STP[par], acc_free[par])
                        sy.dma_start(out=ysb[:, par, :, :],
                                     in_=z3_h[:, :, 512 * w:512 * w + 512]
                                     .rearrange("c p t -> p c t")).then_inc(lr, 16)
                    c["lr"] += 16

                evx_base = (c["ev"], c["vx"])
                post = []
                for oc in range(NC8):
                    pby = pb2 if (oc % 2 == 0) else pb3
                    need = evx_base if oc < 2 else post[oc - 2]

                    @block.tensor
                    def _(t, ei=ei, w=w, oc=oc, pby=pby, par=par, need=need):
                        t.wait_ge(glB, c["glB"])
                        t.wait_ge(lr, c["lr"])
                        t.wait_ge(ev, need[0])
                        t.wait_ge(vx, need[1])
                        for mc in range(16):
                            mi = t.matmul(pby[:, :], wce[:, mc, 128 * oc:128 * oc + 128],
                                          he[:, par, mc, :], start=(mc == 0),
                                          stop=(mc == 15))
                        mi.then_inc(mm, 1)
                    c["mm"] += 1

                    if ei == 0:
                        @block.vector
                        def _(v, w=w, oc=oc, pby=pby, par=par):
                            v.wait_ge(mm, c["mm"])
                            v.wait_ge(lr, c["lr"])
                            v.tensor_tensor(ysb[:, par, oc, :], ysb[:, par, oc, :],
                                            pby[:, :], ALU.add).then_inc(vx, 1)
                        c["vx"] += 1
                    else:
                        @block.vector
                        def _(v, w=w, oc=oc, pby=pby, par=par):
                            v.wait_ge(mm, c["mm"])
                            v.wait_ge(STP[par], acc_free[par])
                            v.tensor_copy(ysb[:, par, oc, :], pby[:, :]).then_inc(ev, 1)
                        c["ev"] += 1
                    post.append((c["ev"], c["vx"]))

                @block.gpsimd
                def _(g, ei=ei, w=w, par=par):
                    g.wait_ge(vx, c["vx"])
                    g.wait_ge(ev, c["ev"])
                    if ei == 0:
                        g.dma_start(out=acc_h[:, :, 512 * w:512 * w + 512]
                                    .rearrange("c p t -> p c t"),
                                    in_=ysb[:, par, :, :]).then_inc(STP[par], 16)
                    else:
                        g.dma_start(out=acc_h[:, :, 512 * w:512 * w + 512]
                                    .rearrange("c p t -> p c t"),
                                    in_=ysb[:, par, :, :],
                                    accum_op=ALU.add).then_inc(STP[par], 16)
                c[STPN[par]] += 16
                acc_free[par] = c[STPN[par]]
                he_free_mm[par] = c["mm"]
                it += 1
            wce_free_mm = c["mm"]

        for w in range(NW):
            @block.sync
            def _(sy, w=w):
                sy.wait_ge(stG, c["stG"])
                sy.wait_ge(stH, c["stH"])
                if w >= 1:
                    sy.wait_ge(stS, c["stS"])
                    sy.wait_ge(mm, c["mm"])
                sy.dma_start(out=zpre[:, :, :],
                             in_=acc_h[:, :, 512 * w:512 * w + 512]
                             .rearrange("c p t -> p c t")).then_inc(lr, 16)
            c["lr"] += 16
            B["zpre_ready"] = (lr, c["lr"])
            emit_ln_stripe(block, B, c, 512 * w, out_h, None)

        @block.sync
        def _(sy):
            sy.wait_ge(stS, c["stS"])
            sy.wait_ge(stG, c["stG"])
            sy.wait_ge(stH, c["stH"])



SEM_NAMES = ["ld", "lr", "mm", "ev", "vx", "glt", "glx", "gl0", "gl1",
             "glA", "glB", "glC", "glD", "glE", "stS", "stG", "stH",
             "cld", "cst"]

_EXT_IN = [
    ("x_fm", [NC8, 128, T], F32),
    ("l0_wq_t", [NC8, 128, D], F32), ("l0_wk_t", [NC8, 128, D], F32),
    ("l0_wv_t", [NC8, 128, D], F32), ("l0_wg_t", [NC8, 128, D], F32),
    ("l0_wo_t", [NC8, 128, D], F32),
    ("l0_lepe_w", [128, NC8, 5], F32), ("l0_lepe_b", [128, NC8], F32),
    ("l0_ropec", [128, S], F32), ("l0_ropes", [128, S], F32),
    ("l0_ropecq", [128, S], F32), ("l0_ropesq", [128, S], F32),
    ("iden_bf", [128, 128], BF16), ("ones_col_bf", [128, 1], BF16),
    ("ones_row_bf", [1, 128], BF16), ("ones_row_f", [1, 128], F32),
    ("sel_bf", [8, 8, 128], BF16),
    ("gate_w", [NC8, 128, 8], F32), ("gate_b", [1, 8], F32),
    ("w1_t", [8, NC8, 128, 4096], F32), ("b1_c", [128, 8, 32], F32),
    ("w2_t", [8, 32, 128, D], F32), ("b2_r", [8, D], F32),
    ("wdq_t", [NC8, 128, 512], F32), ("bdq_c", [128, 4], F32),
    ("wuq_t", [4, 128, D], F32), ("buq_c", [128, NC8], F32),
    ("wqr_t", [4, 128, D], F32), ("bqr_c", [128, NC8], F32),
    ("wkr_t", [NC8, 128, D], F32), ("bkr_c", [128, NC8], F32),
    ("wdkv_t", [NC8, 128, 512], F32), ("bdkv_c", [128, 4], F32),
    ("wuk_t", [4, 128, D], F32), ("buk_c", [128, NC8], F32),
    ("wuv_t", [4, 128, D], F32), ("buv_r", [1, D], F32),
    ("wo1_t", [NC8, 128, D], F32), ("wob_c", [128, NC8], F32),
    ("l1_ropec", [NC8, 128, S], F32), ("l1_ropes", [NC8, 128, S], F32),
    ("rmT_bd", [4, NC8, 128, 128], F32), ("rupT_bd", [4, NC8, 128, 512], F32),
    ("wc_t", [128, 128, D], F32),
]

_INTERNAL = [
    ("z1_fm", [NC8, 128, T], F32), ("z1_bf", [NC8, 128, T], BF16),
    ("acc_b", [NC8, 128, T], F32),
    ("z2_fm", [NC8, 128, T], F32), ("z2_bf", [NC8, 128, T], BF16),
    ("z3_fm", [NC8, 128, T], F32), ("z3_bf", [NC8, 128, T], BF16),
    ("h_d", [128, 128, T], BF16), ("acc_d", [NC8, 128, T], F32),
    ("w2_bf", [8, 32, 128, D], BF16),
]


def emit_barrier(block, sems, c):
    for eng in ("sync", "gpsimd", "scalar", "vector", "tensor"):
        def _f(e):
            for nm in SEM_NAMES:
                e.wait_ge(sems[nm], c[nm])
        getattr(block, eng)(_f)


def build_mega():
    nc = bass.Bass("TRN2", target_bir_lowering=False)
    H = {}
    for nm, shp, dt in _EXT_IN:
        H[nm] = nc.dram_tensor(nm, shp, dt, kind="ExternalInput").ap()
    for nm, shp, dt in _INTERNAL:
        H[nm] = nc.dram_tensor(nm, shp, dt, kind="Internal").ap()
    H["out_fm"] = nc.dram_tensor("out_fm", [NC8, 128, T], F32,
                                 kind="ExternalOutput").ap()
    import contextlib
    es = contextlib.ExitStack()
    PS4 = [es.enter_context(nc.psum_tensor("ps4%d" % j, [128, 2048], F32))
           for j in range(2)]
    P = [PS4[i // 4][:, 512 * (i % 4):512 * (i % 4) + 512] for i in range(8)]
    P = list(P) + PS4
    sems = {nm: es.enter_context(nc.semaphore(nm)) for nm in SEM_NAMES}
    G = {
        "iden": es.enter_context(nc.sbuf_tensor("g_iden", [128, 128], BF16)),
        "ones_col": es.enter_context(nc.sbuf_tensor("g_onc", [128, 1], BF16)),
        "ones_rb": es.enter_context(nc.sbuf_tensor("g_onrb", [1, 128], BF16)),
        "ones_rf": es.enter_context(nc.sbuf_tensor("g_onrf", [1, 128], F32)),
        "selsb": es.enter_context(nc.sbuf_tensor("g_sel", [8, 8, 128], BF16)),
        "ones512": es.enter_context(nc.sbuf_tensor("g_on512", [1, 512], BF16)),
        "epsc": es.enter_context(nc.sbuf_tensor("g_epsc", [1, 1], F32)),
    }
    with es, nc.Block() as block:
        c = {nm: 0 for nm in SEM_NAMES}

        @block.sync
        def _(sy):
            sy.dma_start(out=G["iden"][:, :], in_=H["iden_bf"]).then_inc(sems["ld"], 16)
            sy.dma_start(out=G["ones_col"][:, :], in_=H["ones_col_bf"]).then_inc(sems["ld"], 16)
            sy.dma_start(out=G["ones_rb"][:, :], in_=H["ones_row_bf"]).then_inc(sems["ld"], 16)
            sy.dma_start(out=G["ones_rf"][:, :], in_=H["ones_row_f"]).then_inc(sems["ld"], 16)
            sy.dma_start(out=G["selsb"][:, :, :], in_=H["sel_bf"]).then_inc(sems["ld"], 16)
        c["ld"] += 80
        G["LD0"] = c["ld"]

        @block.vector
        def _(v):
            v.memset(G["ones512"][:, :], 1.0)
            v.memset(G["epsc"][:, :], EPS).then_inc(sems["vx"], 1)
        c["vx"] += 1

        emit_phase_a(nc, block, H, G, sems, P, c)
        emit_barrier(block, sems, c)
        emit_phase_b(nc, block, H, G, sems, P, c)
        emit_barrier(block, sems, c)
        emit_phase_c(nc, block, H, G, sems, P, c)
        emit_barrier(block, sems, c)
        emit_phase_d(nc, block, H, G, sems, P, c)
    return nc


def kernel(**inputs):
    ncores = 8
    sh = prep_shared(inputs)
    x = np.asarray(inputs["L0_attn_hidden_states"])
    cores = list(range(ncores))
    nc = build_mega()
    maps = [dict(sh, x_fm=prep_core_x(x, cc)) for cc in cores]
    res = run_bass_kernel_spmd(nc, maps, core_ids=cores)
    outs = [res.results[cc]["out_fm"] for cc in cores]
    return finish_out(outs)
